# version 13
# baseline (speedup 1.0000x reference)
import contextlib

import numpy as np

import concourse.bass as bass
import concourse.mybir as mybir
from concourse.bass_utils import run_bass_kernel_spmd

F32 = mybir.dt.float32
BF16 = mybir.dt.bfloat16
I32 = mybir.dt.int32
AF = mybir.ActivationFunctionType
ALU = mybir.AluOpType
AX = mybir.AxisListType

B, S, D, NST, KC = 8, 4096, 1024, 32, 4
NDB = D // 128
CH = 512
NCH = S // CH
W = 1024
NWIN = S // W
NGW = NDB * NWIN
LN1024 = float(np.log(1024.0))
EPS = 1e-5


def build_nc():
    nc = bass.Bass("TRN2", target_bir_lowering=False, debug=False)

    x_h = nc.dram_tensor("x", [S, D], F32, kind="ExternalInput").ap()
    Win_h = nc.dram_tensor("W_in", [D, 2 * D], F32, kind="ExternalInput").ap()
    Wdt_h = nc.dram_tensor("W_dt", [D, D], F32, kind="ExternalInput").ap()
    Wout_h = nc.dram_tensor("W_out", [D, D], F32, kind="ExternalInput").ap()
    binc_h = nc.dram_tensor("b_in_c", [128, 16], F32, kind="ExternalInput").ap()
    bdtc_h = nc.dram_tensor("b_dt_c", [128, 8], F32, kind="ExternalInput").ap()
    boutc_h = nc.dram_tensor("b_out_c", [128, 8], F32, kind="ExternalInput").ap()
    cbc_h = nc.dram_tensor("conv_b_c", [128, 8], F32, kind="ExternalInput").ap()
    gc_h = nc.dram_tensor("ln_g_c", [128, 8], F32, kind="ExternalInput").ap()
    lbc_h = nc.dram_tensor("ln_b_c", [128, 8], F32, kind="ExternalInput").ap()
    cw_h = nc.dram_tensor("conv_w_c", [128, 32], F32, kind="ExternalInput").ap()
    alog_h = nc.dram_tensor("A_log_c", [128, 256], F32, kind="ExternalInput").ap()
    wB_h = nc.dram_tensor("wB_c", [128, 256], F32, kind="ExternalInput").ap()
    bB_h = nc.dram_tensor("bB_c", [128, 256], F32, kind="ExternalInput").ap()
    wC_h = nc.dram_tensor("wC_c", [128, 256], F32, kind="ExternalInput").ap()
    bC_h = nc.dram_tensor("bC_c", [128, 256], F32, kind="ExternalInput").ap()
    pk_h = nc.dram_tensor("packed_c", [128, 2048], I32, kind="ExternalInput").ap()
    ws_h = nc.dram_tensor("wscale", [128, 1], F32, kind="ExternalInput").ap()
    fb_h = nc.dram_tensor("ffn_bias", [128, D], F32, kind="ExternalInput").ap()
    id_h = nc.dram_tensor("ident", [128, 128], F32, kind="ExternalInput").ap()

    out_h = nc.dram_tensor("out", [S, D], F32, kind="ExternalOutput").ap()

    dt_s = nc.dram_tensor("dt_s", [D, S], F32, kind="Internal").ap()
    xs_s = nc.dram_tensor("xs_s", [D, S], F32, kind="Internal").ap()
    sz_s = nc.dram_tensor("sz_s", [D, S], F32, kind="Internal").ap()
    v_s = nc.dram_tensor("v_s", [D, S], F32, kind="Internal").ap()
    y_s = nc.dram_tensor("y_s", [D, S], F32, kind="Internal").ap()
    wt_s = nc.dram_tensor("wt_s", [D, D], F32, kind="Internal").ap()

    es = contextlib.ExitStack()
    with es:
        E = es.enter_context

        Wout_bf = [E(nc.sbuf_tensor(f"Woutb{k}", [128, D], BF16)) for k in range(8)]
        tern_bf = [E(nc.sbuf_tensor(f"ternb{k}", [128, D], BF16)) for k in range(8)]
        ident = E(nc.sbuf_tensor("identt", [128, 128], F32))
        ones_t = E(nc.sbuf_tensor("ones_t", [128, 128], F32))
        dum_s = E(nc.sbuf_tensor("dum_s", [128, 128], F32))
        dum_v = E(nc.sbuf_tensor("dum_v", [128, 128], F32))
        binc = E(nc.sbuf_tensor("binc", [128, 16], F32))
        bdtc = E(nc.sbuf_tensor("bdtc", [128, 8], F32))
        boutc = E(nc.sbuf_tensor("boutc", [128, 8], F32))
        cbc = E(nc.sbuf_tensor("cbc", [128, 8], F32))
        gc = E(nc.sbuf_tensor("gc", [128, 8], F32))
        lbc = E(nc.sbuf_tensor("lbc", [128, 8], F32))
        cwc = E(nc.sbuf_tensor("cwc", [128, 32], F32))
        Ac = E(nc.sbuf_tensor("Ac", [128, 256], F32))
        nAc = E(nc.sbuf_tensor("nAc", [128, 256], F32))
        wBc = E(nc.sbuf_tensor("wBc", [128, 256], F32))
        bBc = E(nc.sbuf_tensor("bBc", [128, 256], F32))
        wCc = E(nc.sbuf_tensor("wCc", [128, 256], F32))
        bCc = E(nc.sbuf_tensor("bCc", [128, 256], F32))
        wsc = E(nc.sbuf_tensor("wsc", [128, 4], F32))
        fbb = E(nc.sbuf_tensor("fbb", [128, D], F32))
        cstt = E(nc.sbuf_tensor("cstt", [128, 4], F32))
        fbrow = E(nc.sbuf_tensor("fbrow", [128, D], F32))

        PA = E(nc.psum_tensor("PA", [128, 1024], F32))
        PB = E(nc.psum_tensor("PB", [128, 1024], F32))
        PC = E(nc.psum_tensor("PC", [128, 1024], F32))
        PD = E(nc.psum_tensor("PD", [128, 1024], F32))

        semnames = ["stL", "stV", "stS", "stP", "stG",
                    "xls", "v1a", "s1a", "v2a", "ptr", "vev", "sev", "gvs",
                    "pwi", "sxp", "ssz", "gsz", "pcv", "sxs", "gxs", "vxb", "vhc",
                    "pdt", "sdt", "vdt", "gdt",
                    "p2l", "vpre", "ssem", "visem", "psem", "plsem", "pusem", "vwsem", "gys",
                    "p3l", "vso", "pwo", "vh", "sh2", "psq", "pst", "sst",
                    "pbc", "vx1", "v3s", "s3r", "vq", "pqt", "sqe", "pff",
                    "sff", "pft", "vfin", "gout"]
        SEM = {n: E(nc.semaphore(n)) for n in semnames}

        def w(eng, sem, val):
            if val > 0:
                eng.wait_ge(SEM[sem], val)

        PSLOT = None

        def slot4(m):
            return [PA[:, 0:CH], PA[:, CH:2 * CH],
                    PB[:, 0:CH], PB[:, CH:2 * CH]][m % 4]

        es01 = es.enter_context(contextlib.ExitStack())
        E01 = es01.enter_context
        Win_bf = [E01(nc.sbuf_tensor(f"Winb{k}", [128, 2 * D], BF16))
                  for k in range(8)]
        Wdt_bf = [E01(nc.sbuf_tensor(f"Wdtb{k}", [128, D], BF16))
                  for k in range(8)]

        with contextlib.ExitStack() as es0:
            E0 = es0.enter_context
            wstg = [E0(nc.sbuf_tensor(f"wstg{i}", [128, 2 * D], F32)) for i in range(2)]
            pk_t = E0(nc.sbuf_tensor("pk_t", [128, 2048], I32))
            msk = E0(nc.sbuf_tensor("msk", [128, 4], I32))
            es_t = E0(nc.sbuf_tensor("es_t", [128, 2048], I32))
            cmb = E0(nc.sbuf_tensor("cmb", [128, 8192], F32))
            wot = [E0(nc.sbuf_tensor(f"wot{i}", [128, D], F32)) for i in range(2)]
            blk0 = E0(nc.Block())

            @blk0.sync
            def _(sy):
                loads = [(Win_h[k * 128:(k + 1) * 128, :], 2 * D) for k in range(8)]
                loads += [(Wdt_h[k * 128:(k + 1) * 128, :], D) for k in range(8)]
                loads += [(Wout_h[k * 128:(k + 1) * 128, :], D) for k in range(8)]
                for i, (src, width) in enumerate(loads):
                    if i >= 2:
                        w(sy, "stV", i - 1)
                    sy.dma_start(out=wstg[i % 2][:, 0:width], in_=src)\
                      .then_inc(SEM["stL"], 16)
                for dst, src in [(binc, binc_h), (bdtc, bdtc_h), (boutc, boutc_h),
                                 (cbc, cbc_h), (gc, gc_h), (lbc, lbc_h),
                                 (cwc, cw_h), (Ac, alog_h), (wBc, wB_h),
                                 (bBc, bB_h), (wCc, wC_h), (bCc, bC_h)]:
                    sy.dma_start(out=dst[:], in_=src[:]).then_inc(SEM["stL"], 16)
                sy.dma_start(out=pk_t[:], in_=pk_h[:]).then_inc(SEM["stL"], 16)
                sy.dma_start(out=wsc[:, 0:1], in_=ws_h[:]).then_inc(SEM["stL"], 16)
                sy.dma_start(out=fbb[:], in_=fb_h[:]).then_inc(SEM["stL"], 16)
                sy.dma_start(out=ident[:], in_=id_h[:]).then_inc(SEM["stL"], 16)
                sy.wait_ge(SEM["stG"], 16)
                for o in range(8):
                    if o >= 2:
                        w(sy, "stP", 8 * (o - 1))
                    sy.dma_start(out=wot[o % 2][:],
                                 in_=wt_s[o * 128:(o + 1) * 128, :])\
                      .then_inc(SEM["stL"], 16)

            @blk0.vector
            def _(v):
                tgts = ([(Win_bf[k], 2 * D) for k in range(8)]
                        + [(Wdt_bf[k], D) for k in range(8)]
                        + [(Wout_bf[k], D) for k in range(8)])
                for i, (dst, width) in enumerate(tgts):
                    w(v, "stL", 16 * (i + 1))
                    v.tensor_copy(out=dst[:], in_=wstg[i % 2][:, 0:width])\
                     .then_inc(SEM["stV"], 1)
                w(v, "stL", 16 * 40)
                v.memset(ones_t[:], 1.0)
                v.memset(dum_v[:], 0.0)
                v.memset(cstt[:, 0:1], EPS)
                v.memset(cstt[:, 1:2], 1.0)
                v.memset(cstt[:, 2:3], -LN1024)
                for sft in range(4):
                    v.memset(msk[:, sft:sft + 1], 3 << (2 * sft))
                v.tensor_scalar(out=Ac[:], in0=Ac[:], scalar1=2.0, scalar2=-5.0,
                                op0=ALU.min, op1=ALU.max)
                for sft in range(4):
                    v.tensor_scalar(out=es_t[:], in0=pk_t[:],
                                    scalar1=msk[:, sft:sft + 1], scalar2=None,
                                    op0=ALU.bitwise_and)
                    inst = v.tensor_scalar(
                        out=cmb[:, sft * 2048:(sft + 1) * 2048],
                        in0=es_t[:], scalar1=float(0.25 ** sft), scalar2=-1.0,
                        op0=ALU.mult, op1=ALU.add)
                inst.then_inc(SEM["stV"], 1)
                for idx in range(64):
                    w(v, "stP", idx + 1)
                    ps = (PA if idx % 2 == 0 else PB)
                    o, i = divmod(idx, 8)
                    v.tensor_copy(out=tern_bf[i][:, o * 128:(o + 1) * 128],
                                  in_=ps[:, 0:128]).then_inc(SEM["stV"], 1)

            @blk0.scalar
            def _(s):
                s.wait_ge(SEM["stV"], 25)
                s.activation(out=nAc[:], in_=Ac[:], func=AF.Exp)\
                 .then_inc(SEM["stS"], 1)

            @blk0.tensor
            def _(t):
                for o in range(8):
                    t.wait_ge(SEM["stL"], 16 * (40 + o + 1))
                    for i in range(8):
                        idx = 8 * o + i
                        if idx >= 2:
                            t.wait_ge(SEM["stV"], 25 + idx - 1)
                        ps = (PA if idx % 2 == 0 else PB)
                        t.transpose(ps[:, 0:128],
                                    wot[o % 2][:, i * 128:(i + 1) * 128], ident[:])\
                         .then_inc(SEM["stP"], 1)

            @blk0.gpsimd
            def _(g):
                g.wait_ge(SEM["stV"], 25)
                g.dma_start(
                    out=wt_s.rearrange("(s ph fh) fl -> ph s fh fl", s=4, fh=2),
                    in_=cmb.rearrange("p (s fh fl) -> p s fh fl", s=4, fh=2))\
                 .then_inc(SEM["stG"], 16)

        with contextlib.ExitStack() as es1:
            E1 = es1.enter_context
            x4 = [[E1(nc.sbuf_tensor(f"x4_{s_}_{i}", [128, D], F32))
                   for i in range(4)] for s_ in range(2)]
            cpr = [E1(nc.sbuf_tensor(f"cpr{i}", [128, D], F32)) for i in range(4)]
            stat = E1(nc.sbuf_tensor("stat", [128, 12], F32))
            sq_scr = E1(nc.sbuf_tensor("sq_scr", [128, D], F32))
            xlnT = [E1(nc.sbuf_tensor(f"xlnT{d}", [128, CH], BF16)) for d in range(NDB)]
            vfm = [E1(nc.sbuf_tensor(f"vfm{i}", [128, CH], F32)) for i in range(2)]
            xp = [E1(nc.sbuf_tensor(f"xp{d}", [128, CH + 3], F32)) for d in range(NDB)]
            xcv = [E1(nc.sbuf_tensor(f"xcv{i}", [128, CH], F32)) for i in range(2)]
            szt = [E1(nc.sbuf_tensor(f"szt{i}", [128, CH], F32)) for i in range(2)]
            xsf = [E1(nc.sbuf_tensor(f"xsf{i}", [128, CH], F32)) for i in range(2)]
            xsb = [E1(nc.sbuf_tensor(f"xsb{d}", [128, CH], BF16)) for d in range(NDB)]
            dtf = [E1(nc.sbuf_tensor(f"dtf{i}", [128, CH], F32)) for i in range(2)]
            te_t = E1(nc.sbuf_tensor("te_t", [128, CH], F32))
            sp_t = [E1(nc.sbuf_tensor(f"sp_t{i}", [128, CH], F32)) for i in range(2)]
            blk1 = E1(nc.Block())

            @blk1.sync
            def _(sy):
                for c in range(NCH):
                    if c >= 2:
                        w(sy, "v1a", c - 1)
                    for i in range(4):
                        sy.dma_start(
                            out=x4[c % 2][i][:],
                            in_=x_h[c * CH + i * 128: c * CH + (i + 1) * 128, :])\
                          .then_inc(SEM["xls"], 16)

            @blk1.vector
            def _(v):
                w(v, "stS", 1)
                v.tensor_scalar(out=Ac[:], in0=nAc[:], scalar1=-1.0, scalar2=None,
                                op0=ALU.mult)
                for d in range(NDB):
                    v.memset(xp[d][:, 0:3], 0.0)
                for c in range(NCH):
                    xst = x4[c % 2]
                    w(v, "xls", 64 * (c + 1))
                    for i in range(4):
                        v.reduce_sum(out=stat[:, i:i + 1], in_=xst[i][:], axis=AX.X)
                    for i in range(4):
                        inst = v.tensor_scalar(out=cpr[i][:], in0=xst[i][:],
                                               scalar1=1024.0,
                                               scalar2=stat[:, i:i + 1],
                                               op0=ALU.mult, op1=ALU.subtract)
                    inst.then_inc(SEM["v1a"], 1)
                    w(v, "s1a", c + 1)
                    for i in range(4):
                        inst = v.tensor_scalar(out=cpr[i][:], in0=cpr[i][:],
                                               scalar1=stat[:, 8 + i:9 + i],
                                               scalar2=None, op0=ALU.mult)
                    inst.then_inc(SEM["v2a"], 1)
                    for d in range(NDB):
                        w(v, "ptr", 8 * c + d + 1)
                        ps = (PC if d % 2 == 0 else PD)
                        v.tensor_copy(out=xlnT[d][:], in_=ps[:, 0:CH])\
                         .then_inc(SEM["vev"], 1)
                    for m in range(8):
                        w(v, "pwi", 16 * c + m + 1)
                        v.tensor_scalar(out=xp[m][:, 3:3 + CH], in0=slot4(m),
                                        scalar1=binc[:, m:m + 1], scalar2=None,
                                        op0=ALU.add).then_inc(SEM["sxp"], 1)
                    for d in range(NDB):
                        if 8 * c + d >= 2:
                            w(v, "sxs", 8 * c + d - 1)
                        cv = xcv[d % 2]
                        v.tensor_scalar(out=cv[:], in0=xp[d][:, 0:CH],
                                        scalar1=cwc[:, 4 * d:4 * d + 1],
                                        scalar2=None, op0=ALU.mult)
                        for kk in range(1, 4):
                            inst = v.scalar_tensor_tensor(
                                out=cv[:], in0=xp[d][:, kk:kk + CH],
                                scalar=cwc[:, 4 * d + kk:4 * d + kk + 1],
                                in1=cv[:], op0=ALU.mult, op1=ALU.add)
                        inst.then_inc(SEM["pcv"], 1)
                        v.tensor_copy(out=xp[d][:, 0:3],
                                      in_=xp[d][:, CH:CH + 3])\
                         .then_inc(SEM["vhc"], 1)
                        w(v, "sxs", 8 * c + d + 1)
                        v.tensor_copy(out=xsb[d][:], in_=xsf[d % 2][:])\
                         .then_inc(SEM["vxb"], 1)
                    for m in range(8):
                        w(v, "sdt", 8 * c + m + 1)
                        if 8 * c + m >= 2:
                            w(v, "gdt", 16 * (8 * c + m - 1))
                        v.tensor_scalar(out=dtf[m % 2][:], in0=sp_t[m % 2][:],
                                        scalar1=0.2, scalar2=0.001,
                                        op0=ALU.min, op1=ALU.max)\
                         .then_inc(SEM["vdt"], 1)

            @blk1.scalar
            def _(s):
                def dum():
                    s.activation(out=dum_s[:], in_=ident[:], func=AF.Square)

                for c in range(NCH):
                    w(s, "v1a", c + 1)
                    for i in range(4):
                        s.activation(out=sq_scr[:], in_=cpr[i][:], func=AF.Square,
                                     accum_out=stat[:, 4 + i:5 + i])
                    dum()
                    s.activation(out=stat[:, 8:12], in_=stat[:, 4:8], func=AF.Ln,
                                 scale=float(2.0 ** -30), bias=cstt[:, 0:1])
                    dum()
                    s.activation(out=stat[:, 8:12], in_=stat[:, 8:12],
                                 func=AF.Exp, scale=-0.5, bias=cstt[:, 2:3])\
                     .then_inc(SEM["s1a"], 1)
                    for d in range(NDB):
                        w(s, "ptr", 8 * c + d + 1)
                        w(s, "vev", 8 * c + d + 1)
                        if 8 * c + d >= 2:
                            w(s, "gvs", 16 * (8 * c + d - 1))
                        ps = (PC if d % 2 == 0 else PD)
                        s.activation(out=vfm[d % 2][:], in_=ps[:, 0:CH],
                                     func=AF.Copy).then_inc(SEM["sev"], 1)
                    for m in range(8, 16):
                        w(s, "pwi", 16 * c + m + 1)
                        if 8 * c + (m - 8) >= 2:
                            w(s, "gsz", 16 * (8 * c + (m - 8) - 1))
                        s.activation(out=szt[m % 2][:], in_=slot4(m),
                                     func=AF.Silu, bias=binc[:, m:m + 1])\
                         .then_inc(SEM["ssz"], 1)
                    for d in range(NDB):
                        w(s, "pcv", 8 * c + d + 1)
                        if 8 * c + d >= 2:
                            w(s, "gxs", 16 * (8 * c + d - 1))
                            w(s, "vxb", 8 * c + d - 1)
                        s.activation(out=xsf[d % 2][:], in_=xcv[d % 2][:],
                                     func=AF.Silu,
                                     bias=cbc[:, d:d + 1]).then_inc(SEM["sxs"], 1)
                    for m in range(8):
                        w(s, "pdt", 8 * c + m + 1)
                        if 8 * c + m >= 2:
                            w(s, "vdt", 8 * c + m - 1)
                        s.activation(out=te_t[:], in_=slot4(m), func=AF.Exp,
                                     bias=bdtc[:, m:m + 1])
                        s.activation(out=sp_t[m % 2][:], in_=te_t[:], func=AF.Ln,
                                     bias=cstt[:, 1:2]).then_inc(SEM["sdt"], 1)

            @blk1.tensor
            def _(t):
                for c in range(NCH):
                    w(t, "v2a", c + 1)
                    if c >= 1:
                        w(t, "sxs", 8 * c)
                    for d in range(NDB):
                        if 8 * c + d >= 2:
                            w(t, "sev", 8 * c + d - 1)
                        ps = (PC if d % 2 == 0 else PD)
                        for i in range(4):
                            inst = t.transpose(ps[:, i * 128:(i + 1) * 128],
                                               cpr[i][:, d * 128:(d + 1) * 128],
                                               ident[:])
                        inst.then_inc(SEM["ptr"], 1)
                    w(t, "vev", 8 * (c + 1))
                    w(t, "sev", 8 * (c + 1))
                    for m in range(16):
                        if m >= 4:
                            pm = m - 4
                            if pm < 8:
                                w(t, "sxp", 8 * c + pm + 1)
                            else:
                                w(t, "ssz", 8 * c + (pm - 8) + 1)
                        if c >= 1 and m < 4:
                            w(t, "sdt", 8 * c + (m % 4) - 3)
                        for k in range(8):
                            inst = t.matmul(slot4(m),
                                            Win_bf[k][:, m * 128:(m + 1) * 128],
                                            xlnT[k][:], start=(k == 0),
                                            stop=(k == 7))
                        inst.then_inc(SEM["pwi"], 1)
                    w(t, "vxb", 8 * (c + 1))
                    for m in range(8):
                        if 8 * c + m >= 4:
                            w(t, "sdt", 8 * c + m - 3)
                        w(t, "ssz", 8 * c + 5 + (m % 4))
                        for k in range(8):
                            inst = t.matmul(slot4(m),
                                            Wdt_bf[k][:, m * 128:(m + 1) * 128],
                                            xsb[k][:], start=(k == 0), stop=(k == 7))
                        inst.then_inc(SEM["pdt"], 1)

            @blk1.gpsimd
            def _(g):
                for c in range(NCH):
                    for d in range(NDB):
                        w(g, "sev", 8 * c + d + 1)
                        g.dma_start(out=v_s[d * 128:(d + 1) * 128,
                                           c * CH:(c + 1) * CH],
                                    in_=vfm[d % 2][:]).then_inc(SEM["gvs"], 16)
                    for d in range(NDB):
                        w(g, "ssz", 8 * c + d + 1)
                        g.dma_start(out=sz_s[d * 128:(d + 1) * 128,
                                            c * CH:(c + 1) * CH],
                                    in_=szt[d % 2][:]).then_inc(SEM["gsz"], 16)
                    for d in range(NDB):
                        w(g, "sxs", 8 * c + d + 1)
                        g.dma_start(out=xs_s[d * 128:(d + 1) * 128,
                                            c * CH:(c + 1) * CH],
                                    in_=xsf[d % 2][:]).then_inc(SEM["gxs"], 16)
                    for d in range(NDB):
                        w(g, "vdt", 8 * c + d + 1)
                        g.dma_start(out=dt_s[d * 128:(d + 1) * 128,
                                            c * CH:(c + 1) * CH],
                                    in_=dtf[d % 2][:]).then_inc(SEM["gdt"], 16)

        es01.close()

        with contextlib.ExitStack() as es2:
            E2 = es2.enter_context
            dtw = [E2(nc.sbuf_tensor(f"dtw{i}", [128, W], F32)) for i in range(2)]
            xsw = [E2(nc.sbuf_tensor(f"xsw{i}", [128, W], F32)) for i in range(2)]
            qqw = E2(nc.sbuf_tensor("qqw", [128, W], F32))
            pqw = E2(nc.sbuf_tensor("pqw", [128, W], F32))
            dtcw = E2(nc.sbuf_tensor("dtcw", [128, W], F32))
            a_t = [E2(nc.sbuf_tensor(f"a2_{i}", [128, W], F32)) for i in range(2)]
            pi_t = [E2(nc.sbuf_tensor(f"pi2_{i}", [128, W], F32)) for i in range(2)]
            p_t = [E2(nc.sbuf_tensor(f"p2_{i}", [128, W], F32)) for i in range(2)]
            u_t = E2(nc.sbuf_tensor("u2", [128, W], F32))
            z_t = [E2(nc.sbuf_tensor(f"z2_{i}", [128, W], F32)) for i in range(2)]
            lo_t = E2(nc.sbuf_tensor("lo2", [128, W], F32))
            hi_t = E2(nc.sbuf_tensor("hi2", [128, W], F32))
            d_t = [E2(nc.sbuf_tensor(f"d2_{i}", [128, W], F32)) for i in range(2)]
            pd_t = E2(nc.sbuf_tensor("pd2", [128, W], F32))
            e_t = [E2(nc.sbuf_tensor(f"e2_{i}", [128, W], F32)) for i in range(2)]
            dgCa = [E2(nc.sbuf_tensor(f"dgCa{n}", [128, 128], F32))
                    for n in range(NST)]
            dgBa = [E2(nc.sbuf_tensor(f"dgBa{n}", [128, 128], F32))
                    for n in range(NST)]

            carry = E2(nc.sbuf_tensor("carry", [128, 32], F32))
            yw = [E2(nc.sbuf_tensor("yw0", [128, W], F32))] * 2
            blk2 = E2(nc.Block())

            @blk2.sync
            def _(sy):
                sy.wait_ge(SEM["gdt"], 16 * 64)
                sy.wait_ge(SEM["gxs"], 16 * 64)
                for gw in range(NGW):
                    db, wi = divmod(gw, NWIN)
                    if gw >= 2:
                        w(sy, "visem", 32 * (gw - 1))
                        w(sy, "ssem", 32 * (gw - 1))
                        w(sy, "vwsem", gw - 1)
                    sy.dma_start(out=dtw[gw % 2][:],
                                 in_=dt_s[db * 128:(db + 1) * 128,
                                          wi * W:(wi + 1) * W])\
                      .then_inc(SEM["p2l"], 16)
                    sy.dma_start(out=xsw[gw % 2][:],
                                 in_=xs_s[db * 128:(db + 1) * 128,
                                          wi * W:(wi + 1) * W])\
                      .then_inc(SEM["p2l"], 16)

            @blk2.vector
            def _(v):
                for gw in range(NGW):
                    db, wi = divmod(gw, NWIN)
                    w(v, "p2l", 32 * (gw + 1))
                    w(v, "ssem", 32 * gw)
                    dtc, xsc = dtw[gw % 2], xsw[gw % 2]
                    if wi == 0:
                        w(v, "psem", 32 * gw)
                        for n in range(NST):
                            cn = slice(db * 32 + n, db * 32 + n + 1)
                            v.tensor_scalar(out=dgCa[n][:], in0=ident[:],
                                            scalar1=wCc[:, cn],
                                            scalar2=None, op0=ALU.mult)
                            v.tensor_scalar(out=dgBa[n][:], in0=ident[:],
                                            scalar1=bCc[:, cn],
                                            scalar2=None, op0=ALU.mult)
                    v.tensor_tensor(out=qqw[:], in0=dtc[:], in1=xsc[:], op=ALU.mult)
                    v.tensor_tensor(out=pqw[:], in0=qqw[:], in1=xsc[:], op=ALU.mult)
                    v.tensor_tensor_scan(out=dtcw[:], data0=dtc[:],
                                         data1=dtc[:], initial=0.0,
                                         op0=ALU.add, op1=ALU.max)\
                     .then_inc(SEM["vpre"], 1)
                    for n in range(NST):
                        k = gw * 32 + n
                        cset = k % 2
                        col = slice(db * 32 + n, db * 32 + n + 1)
                        w(v, "ssem", k + 1)
                        if k >= 1:
                            w(v, "psem", k - 1)
                        v.tensor_scalar(out=u_t[:], in0=pqw[:],
                                        scalar1=wBc[:, col], scalar2=None,
                                        op0=ALU.mult)
                        v.scalar_tensor_tensor(out=u_t[:], in0=qqw[:],
                                               scalar=bBc[:, col], in1=u_t[:],
                                               op0=ALU.mult, op1=ALU.add)
                        init = 0.0 if wi == 0 else carry[:, n:n + 1]
                        v.tensor_tensor_scan(out=z_t[cset][:], data0=a_t[cset][:],
                                             data1=u_t[:], initial=init,
                                             op0=ALU.mult, op1=ALU.add)
                        v.scalar_tensor_tensor(out=lo_t[:], in0=z_t[cset][:],
                                               scalar=-10.0, in1=pi_t[cset][:],
                                               op0=ALU.add, op1=ALU.mult)
                        v.scalar_tensor_tensor(out=hi_t[:], in0=z_t[cset][:],
                                               scalar=10.0, in1=pi_t[cset][:],
                                               op0=ALU.add, op1=ALU.mult)
                        v.tensor_tensor_scan(out=d_t[cset][:], data0=lo_t[:],
                                             data1=hi_t[:], initial=0.0,
                                             op0=ALU.max, op1=ALU.min)
                        v.tensor_tensor(out=pd_t[:], in0=p_t[cset][:],
                                        in1=d_t[cset][:], op=ALU.mult)
                        inst = v.tensor_tensor(out=e_t[cset][:], in0=z_t[cset][:],
                                               in1=pd_t[:], op=ALU.subtract)
                        if wi < NWIN - 1:
                            v.tensor_copy(out=dum_v[:, 0:32], in_=ident[:, 0:32])
                            inst = v.scalar_tensor_tensor(
                                out=carry[:, n:n + 1], in0=z_t[cset][:, W - 1:W],
                                scalar=1.0, in1=pd_t[:, W - 1:W],
                                op0=ALU.mult, op1=ALU.subtract)
                        inst.then_inc(SEM["visem"], 1)
                    w(v, "psem", 32 * (gw + 1))
                    if gw >= 1:
                        w(v, "gys", 16 * gw)
                    y1ps = (PA if gw % 2 == 0 else PC)
                    y2ps = (PB if gw % 2 == 0 else PD)
                    v.tensor_tensor(out=yw[gw % 2][:], in0=xsc[:], in1=y1ps[:],
                                    op=ALU.mult)
                    v.tensor_tensor(out=yw[gw % 2][:], in0=yw[gw % 2][:],
                                    in1=y2ps[:], op=ALU.add)\
                     .then_inc(SEM["vwsem"], 1)

            @blk2.scalar
            def _(s):
                for gw in range(NGW):
                    db, wi = divmod(gw, NWIN)
                    w(s, "vpre", gw + 1)
                    dtc = dtw[gw % 2]
                    for n in range(NST):
                        k = gw * 32 + n
                        cset = k % 2
                        col = slice(db * 32 + n, db * 32 + n + 1)
                        if k >= 2:
                            w(s, "visem", k - 1)
                        s.activation(out=a_t[cset][:], in_=dtc[:], func=AF.Exp,
                                     scale=Ac[:, col])
                        s.activation(out=pi_t[cset][:], in_=dtcw[:], func=AF.Exp,
                                     scale=nAc[:, col])
                        s.activation(out=p_t[cset][:], in_=dtcw[:], func=AF.Exp,
                                     scale=Ac[:, col]).then_inc(SEM["ssem"], 1)

            @blk2.tensor
            def _(t):
                for gw in range(NGW):
                    for n in range(NST):
                        k = gw * 32 + n
                        cset = k % 2
                        t.wait_ge(SEM["visem"], k + 1)
                        if n == 0 and gw > 1:
                            t.wait_ge(SEM["vwsem"], gw - 1)
                        y1ps = (PA if gw % 2 == 0 else PC)
                        y2ps = (PB if gw % 2 == 0 else PD)
                        first, last = (n == 0), (n == NST - 1)
                        t.matmul(y1ps[:, 0:512], dgCa[n][:], e_t[cset][:, 0:512],
                                 start=first, stop=last)
                        t.matmul(y1ps[:, 512:1024], dgCa[n][:],
                                 e_t[cset][:, 512:1024],
                                 start=first, stop=last)
                        t.matmul(y2ps[:, 0:512], dgBa[n][:], e_t[cset][:, 0:512],
                                 start=first, stop=last)
                        t.matmul(y2ps[:, 512:1024], dgBa[n][:],
                                 e_t[cset][:, 512:1024],
                                 start=first, stop=last).then_inc(SEM["psem"], 1)

            @blk2.gpsimd
            def _(g):
                for gw in range(NGW):
                    db, wi = divmod(gw, NWIN)
                    w(g, "vwsem", gw + 1)
                    g.dma_start(out=y_s[db * 128:(db + 1) * 128,
                                        wi * W:(wi + 1) * W], in_=yw[gw % 2][:])\
                     .then_inc(SEM["gys"], 16)

        with contextlib.ExitStack() as es3:
            E3 = es3.enter_context
            yfm = [E3(nc.sbuf_tensor(f"yfm{d}", [128, CH], F32)) for d in range(NDB)]
            szf = [E3(nc.sbuf_tensor(f"szf{d}", [128, CH], F32)) for d in range(NDB)]
            vf3 = [E3(nc.sbuf_tensor(f"vf3{d}", [128, CH], F32)) for d in range(NDB)]
            x43 = [E3(nc.sbuf_tensor(f"x43_{i}", [128, D], F32)) for i in range(4)]
            sob = [E3(nc.sbuf_tensor(f"sob{d}", [128, CH], BF16)) for d in range(NDB)]
            hfm = [E3(nc.sbuf_tensor(f"hfm{d}", [128, CH], F32)) for d in range(NDB)]
            h2f = [E3(nc.sbuf_tensor(f"h2f{i}", [128, CH], F32)) for i in range(2)]
            tmp3 = E3(nc.sbuf_tensor("tmp3", [128, CH], F32))
            st3 = E3(nc.sbuf_tensor("st3", [128, 28], F32))
            x1t = [E3(nc.sbuf_tensor(f"x1t{i}", [128, D], F32)) for i in range(4)]
            c3 = [E3(nc.sbuf_tensor(f"c3_{i}", [128, D], F32)) for i in range(4)]
            qi = E3(nc.sbuf_tensor("qi", [128, D], I32))
            qfm = [E3(nc.sbuf_tensor(f"qfm{d}", [128, CH], BF16)) for d in range(NDB)]
            ffm = [E3(nc.sbuf_tensor(f"ffm{d}", [128, CH], F32)) for d in range(NDB)]
            sq3 = E3(nc.sbuf_tensor("sq3", [128, D], F32))
            x2l = c3
            qf = c3
            xb3 = x43
            otl = c3
            blk3 = E3(nc.Block())

            @blk3.sync
            def _(sy):
                sy.wait_ge(SEM["gys"], 16 * 32)
                sy.wait_ge(SEM["gvs"], 16 * 64)
                sy.wait_ge(SEM["gsz"], 16 * 64)
                for c in range(NCH):
                    if c >= 1:
                        w(sy, "vfin", 4 * c)
                    for d in range(NDB):
                        sy.dma_start(out=yfm[d][:],
                                     in_=y_s[d * 128:(d + 1) * 128,
                                             c * CH:(c + 1) * CH])\
                          .then_inc(SEM["p3l"], 16)
                    for d in range(NDB):
                        sy.dma_start(out=szf[d][:],
                                     in_=sz_s[d * 128:(d + 1) * 128,
                                              c * CH:(c + 1) * CH])\
                          .then_inc(SEM["p3l"], 16)
                    for d in range(NDB):
                        sy.dma_start(out=vf3[d][:],
                                     in_=v_s[d * 128:(d + 1) * 128,
                                             c * CH:(c + 1) * CH])\
                          .then_inc(SEM["p3l"], 16)
                    for i in range(4):
                        sy.dma_start(
                            out=x43[i][:],
                            in_=x_h[c * CH + i * 128: c * CH + (i + 1) * 128, :])\
                          .then_inc(SEM["p3l"], 16)

            @blk3.vector
            def _(v):
                def dum():
                    v.tensor_copy(out=dum_v[:], in_=ident[:])

                for c in range(NCH):
                    w(v, "p3l", 16 * 28 * (c + 1))
                    for d in range(NDB):
                        inst = v.tensor_tensor(out=sob[d][:], in0=yfm[d][:],
                                               in1=szf[d][:], op=ALU.mult)
                    inst.then_inc(SEM["vso"], 1)
                    for m in range(8):
                        w(v, "pwo", 8 * c + m + 1)
                        v.scalar_tensor_tensor(out=hfm[m][:], in0=slot4(m),
                                               scalar=boutc[:, m:m + 1],
                                               in1=vf3[m][:], op0=ALU.add,
                                               op1=ALU.add).then_inc(SEM["vh"], 1)
                    w(v, "pst", c + 1)
                    v.tensor_scalar(out=tmp3[0:1, 0:CH], in0=PC[0:1, 0:CH],
                                    scalar1=float(2.0 ** -10), scalar2=None,
                                    op0=ALU.mult)
                    v.tensor_scalar(out=tmp3[32:33, 0:CH], in0=PC[0:1, 0:CH],
                                    scalar1=0.03125, scalar2=None,
                                    op0=ALU.mult).then_inc(SEM["vx1"], 1)
                    w(v, "sst", 2 * c + 1)
                    w(v, "psq", 8 * (c + 1))
                    v.tensor_tensor(out=tmp3[32:33, 0:CH],
                                    in0=PC[0:1, CH:2 * CH],
                                    in1=tmp3[64:65, 0:CH],
                                    op=ALU.subtract).then_inc(SEM["vx1"], 1)
                    w(v, "pbc", c + 1)
                    for d in range(NDB):
                        v.tensor_tensor(out=tmp3[:], in0=hfm[d][:],
                                        in1=PD[:, 0:CH], op=ALU.subtract)
                        v.tensor_tensor(out=tmp3[:], in0=tmp3[:],
                                        in1=PD[:, CH:2 * CH], op=ALU.mult)
                        v.tensor_scalar(out=hfm[d][:], in0=tmp3[:],
                                        scalar1=gc[:, d:d + 1],
                                        scalar2=lbc[:, d:d + 1],
                                        op0=ALU.mult, op1=ALU.add)\
                         .then_inc(SEM["vh"], 1)
                    for i in range(4):
                        w(v, "pqt", 12 * c + i + 1)
                        v.tensor_tensor(out=x1t[i][:], in0=PC[:], in1=x43[i][:],
                                        op=ALU.add).then_inc(SEM["vx1"], 1)
                    for i in range(4):
                        v.reduce_sum(out=st3[:, i:i + 1], in_=x1t[i][:], axis=AX.X)
                    for i in range(4):
                        inst = v.tensor_scalar(out=c3[i][:], in0=x1t[i][:],
                                               scalar1=1024.0,
                                               scalar2=st3[:, i:i + 1],
                                               op0=ALU.mult, op1=ALU.subtract)
                    inst.then_inc(SEM["v3s"], 1)
                    w(v, "s3r", c + 1)
                    for i in range(4):
                        v.tensor_scalar(out=x2l[i][:], in0=c3[i][:],
                                        scalar1=st3[:, 20 + i:21 + i],
                                        scalar2=None, op0=ALU.mult)
                    for i in range(4):
                        v.reduce_max(out=st3[:, 4 + i:5 + i], in_=x2l[i][:],
                                     axis=AX.X, apply_absolute_value=True)
                    for i in range(4):
                        v.tensor_tensor(out=xb3[i][:], in0=x1t[i][:],
                                        in1=fbb[:], op=ALU.add)
                    v.tensor_scalar(out=st3[:, 4:8], in0=st3[:, 4:8],
                                    scalar1=1e-5, scalar2=None, op0=ALU.max)
                    dum()
                    dum()
                    v.reciprocal(out=st3[:, 8:12], in_=st3[:, 4:8])
                    v.tensor_scalar(out=st3[:, 12:16], in0=st3[:, 4:8],
                                    scalar1=float(1.0 / 127.0), scalar2=None,
                                    op0=ALU.mult)
                    dum()
                    v.tensor_scalar(out=st3[:, 8:12], in0=st3[:, 8:12],
                                    scalar1=127.0, scalar2=None, op0=ALU.mult)
                    dum()
                    dum()
                    for i in range(4):
                        v.tensor_scalar(out=qf[i][:], in0=x2l[i][:],
                                        scalar1=st3[:, 8 + i:9 + i],
                                        scalar2=None, op0=ALU.mult)
                        v.tensor_copy(out=qi[:], in_=qf[i][:])
                        inst = v.tensor_copy(out=qf[i][:], in_=qi[:])
                    inst.then_inc(SEM["vq"], 1)
                    v.tensor_scalar(out=st3[:, 12:16], in0=st3[:, 12:16],
                                    scalar1=wsc[:, 0:1], scalar2=None,
                                    op0=ALU.mult)
                    for i in range(4):
                        w(v, "pft", 4 * c + i + 1)
                        v.tensor_scalar(out=otl[i][:], in0=PD[:],
                                        scalar1=st3[:, 12 + i:13 + i],
                                        scalar2=None, op0=ALU.mult)
                        v.tensor_tensor(out=otl[i][:], in0=otl[i][:],
                                        in1=xb3[i][:], op=ALU.add)\
                         .then_inc(SEM["vfin"], 1)

            @blk3.scalar
            def _(s):
                def dum():
                    s.activation(out=dum_s[:], in_=ident[:], func=AF.Square)

                for c in range(NCH):
                    for m in range(8):
                        w(s, "vh", 16 * c + m + 1)
                        if 8 * c + m >= 2:
                            w(s, "psq", 8 * c + m - 1)
                        s.activation(out=h2f[m % 2][:], in_=hfm[m][:],
                                     func=AF.Square).then_inc(SEM["sh2"], 1)
                    w(s, "vx1", 6 * c + 1)
                    s.activation(out=tmp3[64:65, 0:CH],
                                 in_=tmp3[32:33, 0:CH], func=AF.Square)\
                     .then_inc(SEM["sst"], 1)
                    w(s, "vx1", 6 * c + 2)
                    s.activation(out=tmp3[32:33, 0:CH],
                                 in_=tmp3[32:33, 0:CH], func=AF.Ln,
                                 scale=float(2.0 ** -10), bias=cstt[0:1, 0:1])
                    s.activation(out=tmp3[32:33, 0:CH],
                                 in_=tmp3[32:33, 0:CH], func=AF.Exp,
                                 scale=-0.5).then_inc(SEM["sst"], 1)
                    w(s, "v3s", c + 1)
                    for i in range(4):
                        s.activation(out=sq3[:], in_=c3[i][:], func=AF.Square,
                                     accum_out=st3[:, 16 + i:17 + i])
                    dum()
                    s.activation(out=st3[:, 20:24], in_=st3[:, 16:20], func=AF.Ln,
                                 scale=float(2.0 ** -30), bias=cstt[:, 0:1])
                    dum()
                    s.activation(out=st3[:, 20:24], in_=st3[:, 20:24], func=AF.Exp,
                                 scale=-0.5, bias=cstt[:, 2:3]).then_inc(SEM["s3r"], 1)
                    for d in range(NDB):
                        w(s, "pqt", 12 * c + 4 + d + 1)
                        ps = (PA if d % 2 == 0 else PB)
                        s.activation(out=qfm[d][:], in_=ps[:, 0:CH], func=AF.Copy)\
                         .then_inc(SEM["sqe"], 1)
                    for m in range(8):
                        w(s, "pff", 8 * c + m + 1)
                        if c >= 1 and m < 1:
                            w(s, "pft", 4 * c)
                        s.activation(out=ffm[m][:], in_=slot4(m), func=AF.Copy)\
                         .then_inc(SEM["sff"], 1)

            @blk3.tensor
            def _(t):
                for c in range(NCH):
                    t.wait_ge(SEM["vso"], c + 1)
                    if c >= 1:
                        t.wait_ge(SEM["vfin"], 4 * c)
                        t.wait_ge(SEM["sff"], 8 * c)
                    for m in range(8):
                        if 8 * c + m >= 4:
                            t.wait_ge(SEM["vh"], 16 * c + m - 3)
                        for k in range(8):
                            inst = t.matmul(slot4(m),
                                            Wout_bf[k][:, m * 128:(m + 1) * 128],
                                            sob[k][:], start=(k == 0), stop=(k == 7))
                        inst.then_inc(SEM["pwo"], 1)
                    for d in range(NDB):
                        t.wait_ge(SEM["vh"], 16 * c + d + 1)
                        inst = t.matmul(PC[0:1, 0:CH], ones_t[:, 0:1], hfm[d][:],
                                        start=(d == 0), stop=(d == 7))
                    inst.then_inc(SEM["pst"], 1)
                    for d in range(NDB):
                        t.wait_ge(SEM["sh2"], 8 * c + d + 1)
                        inst = t.matmul(PC[0:1, CH:2 * CH], ones_t[:, 0:1],
                                        h2f[d % 2][:], start=(d == 0),
                                        stop=(d == 7))
                        inst.then_inc(SEM["psq"], 1)
                    t.wait_ge(SEM["sst"], 2 * (c + 1))
                    t.matmul(PD[:, 0:CH], ones_t[0:1, 0:128], tmp3[0:1, 0:CH],
                             start=True, stop=True)
                    t.matmul(PD[:, CH:2 * CH], ones_t[32:33, 0:128],
                             tmp3[32:33, 0:CH], start=True, stop=True)\
                     .then_inc(SEM["pbc"], 1)
                    t.wait_ge(SEM["vh"], 16 * c + 16)
                    for i in range(4):
                        t.wait_ge(SEM["vx1"], 6 * c + 2 + i)
                        for d in range(NDB):
                            inst = t.transpose(PC[:, d * 128:(d + 1) * 128],
                                               hfm[d][:, i * 128:(i + 1) * 128],
                                               ident[:])
                        inst.then_inc(SEM["pqt"], 1)
                    t.wait_ge(SEM["vq"], c + 1)
                    for d in range(NDB):
                        if 8 * c + d >= 2:
                            t.wait_ge(SEM["sqe"], 8 * c + d - 1)
                        ps = (PA if d % 2 == 0 else PB)
                        for i in range(4):
                            inst = t.transpose(ps[:, i * 128:(i + 1) * 128],
                                               qf[i][:, d * 128:(d + 1) * 128],
                                               ident[:])
                        inst.then_inc(SEM["pqt"], 1)
                    t.wait_ge(SEM["sqe"], 8 * (c + 1))
                    for m in range(8):
                        if 8 * c + m >= 4:
                            t.wait_ge(SEM["sff"], 8 * c + m - 3)
                        for k in range(8):
                            inst = t.matmul(slot4(m),
                                            tern_bf[k][:, m * 128:(m + 1) * 128],
                                            qfm[k][:], start=(k == 0), stop=(k == 7))
                        inst.then_inc(SEM["pff"], 1)
                    t.wait_ge(SEM["sff"], 8 * (c + 1))
                    for i in range(4):
                        if 4 * c + i >= 1:
                            t.wait_ge(SEM["vfin"], 4 * c + i)
                        for d in range(NDB):
                            inst = t.transpose(PD[:, d * 128:(d + 1) * 128],
                                               ffm[d][:, i * 128:(i + 1) * 128],
                                               ident[:])
                        inst.then_inc(SEM["pft"], 1)

            @blk3.gpsimd
            def _(g):
                for c in range(NCH):
                    for i in range(4):
                        w(g, "vfin", 4 * c + i + 1)
                        g.dma_start(
                            out=out_h[c * CH + i * 128: c * CH + (i + 1) * 128, :],
                            in_=otl[i][:]).then_inc(SEM["gout"], 16)
                g.wait_ge(SEM["gout"], 16 * 4 * NCH)

    return nc


_CACHE = {}


def kernel(**inputs):
    x = np.asarray(inputs["L0_attn_x"], dtype=np.float32)
    W_in = np.ascontiguousarray(np.asarray(inputs["L0_attn_W_in"], np.float32))
    W_dt = np.ascontiguousarray(np.asarray(inputs["L0_attn_W_dt"], np.float32))
    W_out = np.ascontiguousarray(np.asarray(inputs["L0_attn_W_out"], np.float32))

    def colsplit(vec, nb):
        return np.ascontiguousarray(
            np.asarray(vec, dtype=np.float32).reshape(nb, 128).T)

    def colsplit3(m):
        a = np.asarray(m, dtype=np.float32).reshape(NDB, 128, -1)
        return np.ascontiguousarray(
            a.transpose(1, 0, 2).reshape(128, -1))

    consts = dict(
        W_in=W_in, W_dt=W_dt, W_out=W_out,
        b_in_c=colsplit(inputs["L0_attn_b_in"], 16),
        b_dt_c=colsplit(inputs["L0_attn_b_dt"], 8),
        b_out_c=colsplit(inputs["L0_attn_b_out"], 8),
        conv_b_c=colsplit(inputs["L0_attn_conv_b"], 8),
        ln_g_c=colsplit(inputs["L0_attn_ln_g"], 8),
        ln_b_c=colsplit(inputs["L0_attn_ln_b"], 8),
        conv_w_c=colsplit3(np.asarray(inputs["L0_attn_conv_w"],
                                      np.float32).reshape(D, KC)),
        A_log_c=colsplit3(inputs["L0_attn_A_log"]),
        wB_c=colsplit3(inputs["L0_attn_wB"]),
        bB_c=colsplit3(inputs["L0_attn_bB"]),
        wC_c=colsplit3(inputs["L0_attn_wC"]),
        bC_c=colsplit3(inputs["L0_attn_bC"]),
        packed_c=np.ascontiguousarray(
            np.asarray(inputs["L0_ffn_packed_weight"],
                       np.int32).reshape(128, 2048)),
        wscale=np.repeat(np.asarray(inputs["L0_ffn_weight_scale"],
                                    np.float32).reshape(1, 1), 128, axis=0),
        ffn_bias=np.repeat(np.asarray(inputs["L0_ffn_bias"],
                                      np.float32).reshape(1, D), 128, axis=0),
        ident=np.eye(128, dtype=np.float32),
    )

    if "nc" not in _CACHE:
        _CACHE["nc"] = build_nc()
    nc = _CACHE["nc"]

    in_maps = [dict(consts, x=np.ascontiguousarray(x[b])) for b in range(B)]
    res = run_bass_kernel_spmd(nc, in_maps, core_ids=list(range(B)))
    out = np.stack([np.asarray(res.results[b]["out"]) for b in range(B)], axis=0)
    return out.astype(np.float32)


# version 15
# speedup vs baseline: 1.0177x; 1.0177x over previous
import contextlib

import numpy as np

import concourse.bass as bass
import concourse.mybir as mybir
from concourse.bass_utils import run_bass_kernel_spmd

F32 = mybir.dt.float32
BF16 = mybir.dt.bfloat16
I32 = mybir.dt.int32
AF = mybir.ActivationFunctionType
ALU = mybir.AluOpType
AX = mybir.AxisListType

B, S, D, NST, KC = 8, 4096, 1024, 32, 4
NDB = D // 128
CH = 512
NCH = S // CH
W = 1024
NWIN = S // W
NGW = NDB * NWIN
LN1024 = float(np.log(1024.0))
EPS = 1e-5


def build_nc():
    nc = bass.Bass("TRN2", target_bir_lowering=False, debug=False)

    x_h = nc.dram_tensor("x", [S, D], F32, kind="ExternalInput").ap()
    Win_h = nc.dram_tensor("W_in", [D, 2 * D], F32, kind="ExternalInput").ap()
    Wdt_h = nc.dram_tensor("W_dt", [D, D], F32, kind="ExternalInput").ap()
    Wout_h = nc.dram_tensor("W_out", [D, D], F32, kind="ExternalInput").ap()
    binc_h = nc.dram_tensor("b_in_c", [128, 16], F32, kind="ExternalInput").ap()
    bdtc_h = nc.dram_tensor("b_dt_c", [128, 8], F32, kind="ExternalInput").ap()
    boutc_h = nc.dram_tensor("b_out_c", [128, 8], F32, kind="ExternalInput").ap()
    cbc_h = nc.dram_tensor("conv_b_c", [128, 8], F32, kind="ExternalInput").ap()
    gc_h = nc.dram_tensor("ln_g_c", [128, 8], F32, kind="ExternalInput").ap()
    lbc_h = nc.dram_tensor("ln_b_c", [128, 8], F32, kind="ExternalInput").ap()
    cw_h = nc.dram_tensor("conv_w_c", [128, 32], F32, kind="ExternalInput").ap()
    alog_h = nc.dram_tensor("A_log_c", [128, 256], F32, kind="ExternalInput").ap()
    wB_h = nc.dram_tensor("wB_c", [128, 256], F32, kind="ExternalInput").ap()
    bB_h = nc.dram_tensor("bB_c", [128, 256], F32, kind="ExternalInput").ap()
    wC_h = nc.dram_tensor("wC_c", [128, 256], F32, kind="ExternalInput").ap()
    bC_h = nc.dram_tensor("bC_c", [128, 256], F32, kind="ExternalInput").ap()
    pk_h = nc.dram_tensor("packed_c", [128, 2048], I32, kind="ExternalInput").ap()
    ws_h = nc.dram_tensor("wscale", [128, 1], F32, kind="ExternalInput").ap()
    fb_h = nc.dram_tensor("ffn_bias", [128, D], F32, kind="ExternalInput").ap()
    id_h = nc.dram_tensor("ident", [128, 128], F32, kind="ExternalInput").ap()

    out_h = nc.dram_tensor("out", [S, D], F32, kind="ExternalOutput").ap()

    dt_s = nc.dram_tensor("dt_s", [D, S], F32, kind="Internal").ap()
    xs_s = nc.dram_tensor("xs_s", [D, S], F32, kind="Internal").ap()
    sz_s = nc.dram_tensor("sz_s", [D, S], F32, kind="Internal").ap()
    v_s = nc.dram_tensor("v_s", [D, S], F32, kind="Internal").ap()
    y_s = nc.dram_tensor("y_s", [D, S], F32, kind="Internal").ap()
    wt_s = nc.dram_tensor("wt_s", [D, D], F32, kind="Internal").ap()

    es = contextlib.ExitStack()
    with es:
        E = es.enter_context

        Wout_bf = [E(nc.sbuf_tensor(f"Woutb{k}", [128, D], BF16)) for k in range(8)]
        tern_bf = [E(nc.sbuf_tensor(f"ternb{k}", [128, D], BF16)) for k in range(8)]
        ident = E(nc.sbuf_tensor("identt", [128, 128], F32))
        ones_t = E(nc.sbuf_tensor("ones_t", [128, 128], F32))
        dum_s = E(nc.sbuf_tensor("dum_s", [128, 128], F32))
        dum_v = E(nc.sbuf_tensor("dum_v", [128, 128], F32))
        binc = E(nc.sbuf_tensor("binc", [128, 16], F32))
        bdtc = E(nc.sbuf_tensor("bdtc", [128, 8], F32))
        boutc = E(nc.sbuf_tensor("boutc", [128, 8], F32))
        cbc = E(nc.sbuf_tensor("cbc", [128, 8], F32))
        gc = E(nc.sbuf_tensor("gc", [128, 8], F32))
        lbc = E(nc.sbuf_tensor("lbc", [128, 8], F32))
        cwc = E(nc.sbuf_tensor("cwc", [128, 32], F32))
        Ac = E(nc.sbuf_tensor("Ac", [128, 256], F32))
        nAc = E(nc.sbuf_tensor("nAc", [128, 256], F32))
        wBc = E(nc.sbuf_tensor("wBc", [128, 256], F32))
        bBc = E(nc.sbuf_tensor("bBc", [128, 256], F32))
        wCc = E(nc.sbuf_tensor("wCc", [128, 256], F32))
        bCc = E(nc.sbuf_tensor("bCc", [128, 256], F32))
        wsc = E(nc.sbuf_tensor("wsc", [128, 4], F32))
        fbb = E(nc.sbuf_tensor("fbb", [128, D], F32))
        cstt = E(nc.sbuf_tensor("cstt", [128, 4], F32))
        fbrow = E(nc.sbuf_tensor("fbrow", [128, D], F32))

        PA = E(nc.psum_tensor("PA", [128, 1024], F32))
        PB = E(nc.psum_tensor("PB", [128, 1024], F32))
        PC = E(nc.psum_tensor("PC", [128, 1024], F32))
        PD = E(nc.psum_tensor("PD", [128, 1024], F32))

        semnames = ["stL", "stV", "stS", "stP", "stG", "wgL0", "wgL1",
                    "wtP0", "wtP1",
                    "xls", "v1a", "s1a", "v2a", "ptr", "vev", "sev", "gvs",
                    "pwi", "sxp", "ssz", "gsz", "pcv", "sxs", "gxs", "vxb", "vhc",
                    "pdt", "sdt", "vdt", "gdt",
                    "gvs0", "gvs1", "gsz0", "gsz1", "gxs0", "gxs1",
                    "gdt0", "gdt1", "gys0", "gys1",
                    "p3y", "p3s", "p3v", "p3x",
                    "p2l", "vpre", "ssem", "visem", "psem", "plsem", "pusem", "vwsem", "gys",
                    "p3l", "vso", "pwo", "vh", "sh2", "psq", "pst", "sst",
                    "pbc", "vx1", "v3s", "s3r", "vq", "pqt", "sqe", "pff",
                    "sff", "pft", "vfin", "gout"]
        SEM = {n: E(nc.semaphore(n)) for n in semnames}

        def w(eng, sem, val):
            if val > 0:
                eng.wait_ge(SEM[sem], val)

        PSLOT = None

        def slot4(m):
            return [PA[:, 0:CH], PA[:, CH:2 * CH],
                    PB[:, 0:CH], PB[:, CH:2 * CH]][m % 4]

        es01 = es.enter_context(contextlib.ExitStack())
        E01 = es01.enter_context
        Win_bf = [E01(nc.sbuf_tensor(f"Winb{k}", [128, 2 * D], BF16))
                  for k in range(8)]
        Wdt_bf = [E01(nc.sbuf_tensor(f"Wdtb{k}", [128, D], BF16))
                  for k in range(8)]

        with contextlib.ExitStack() as es0:
            E0 = es0.enter_context
            wstg = [E0(nc.sbuf_tensor(f"wstg{i}", [128, 2 * D], F32)) for i in range(2)]
            pk_t = E0(nc.sbuf_tensor("pk_t", [128, 2048], I32))
            msk = E0(nc.sbuf_tensor("msk", [128, 4], I32))
            es_t = E0(nc.sbuf_tensor("es_t", [128, 2048], I32))
            cmb = E0(nc.sbuf_tensor("cmb", [128, 8192], F32))
            wot = [E0(nc.sbuf_tensor(f"wot{i}", [128, D], F32)) for i in range(2)]
            blk0 = E0(nc.Block())

            @blk0.sync
            def _(sy):
                loads = [(Win_h[k * 128:(k + 1) * 128, :], 2 * D) for k in range(8)]
                loads += [(Wdt_h[k * 128:(k + 1) * 128, :], D) for k in range(8)]
                loads += [(Wout_h[k * 128:(k + 1) * 128, :], D) for k in range(8)]
                for i, (src, width) in enumerate(loads):
                    if i >= 2:
                        w(sy, "stV", i - 1)
                    sy.dma_start(out=wstg[i % 2][:, 0:width], in_=src)\
                      .then_inc(SEM["wgL0" if i % 2 == 0 else "wgL1"], 16)
                for dst, src in [(binc, binc_h), (bdtc, bdtc_h), (boutc, boutc_h),
                                 (cbc, cbc_h), (gc, gc_h), (lbc, lbc_h),
                                 (cwc, cw_h), (Ac, alog_h), (wBc, wB_h),
                                 (bBc, bB_h), (wCc, wC_h), (bCc, bC_h)]:
                    sy.dma_start(out=dst[:], in_=src[:]).then_inc(SEM["stL"], 16)
                sy.dma_start(out=pk_t[:], in_=pk_h[:]).then_inc(SEM["stL"], 16)
                sy.dma_start(out=wsc[:, 0:1], in_=ws_h[:]).then_inc(SEM["stL"], 16)
                sy.dma_start(out=fbb[:], in_=fb_h[:]).then_inc(SEM["stL"], 16)
                sy.dma_start(out=ident[:], in_=id_h[:]).then_inc(SEM["stL"], 16)
                sy.wait_ge(SEM["stG"], 16)
                for o in range(8):
                    if o >= 2:
                        w(sy, "stP", 8 * (o - 1))
                    sy.dma_start(out=wot[o % 2][:],
                                 in_=wt_s[o * 128:(o + 1) * 128, :])\
                      .then_inc(SEM["wtP0" if o % 2 == 0 else "wtP1"], 16)

            @blk0.vector
            def _(v):
                tgts = ([(Win_bf[k], 2 * D) for k in range(8)]
                        + [(Wdt_bf[k], D) for k in range(8)]
                        + [(Wout_bf[k], D) for k in range(8)])
                for i, (dst, width) in enumerate(tgts):
                    w(v, "wgL0" if i % 2 == 0 else "wgL1", 16 * (i // 2 + 1))
                    v.tensor_copy(out=dst[:], in_=wstg[i % 2][:, 0:width])\
                     .then_inc(SEM["stV"], 1)
                w(v, "stL", 16 * 16)
                v.memset(ones_t[:], 1.0)
                v.memset(dum_v[:], 0.0)
                v.memset(cstt[:, 0:1], EPS)
                v.memset(cstt[:, 1:2], 1.0)
                v.memset(cstt[:, 2:3], -LN1024)
                for sft in range(4):
                    v.memset(msk[:, sft:sft + 1], 3 << (2 * sft))
                v.tensor_scalar(out=Ac[:], in0=Ac[:], scalar1=2.0, scalar2=-5.0,
                                op0=ALU.min, op1=ALU.max)
                for sft in range(4):
                    v.tensor_scalar(out=es_t[:], in0=pk_t[:],
                                    scalar1=msk[:, sft:sft + 1], scalar2=None,
                                    op0=ALU.bitwise_and)
                    inst = v.tensor_scalar(
                        out=cmb[:, sft * 2048:(sft + 1) * 2048],
                        in0=es_t[:], scalar1=float(0.25 ** sft), scalar2=-1.0,
                        op0=ALU.mult, op1=ALU.add)
                inst.then_inc(SEM["stV"], 1)
                for idx in range(64):
                    w(v, "stP", idx + 1)
                    ps = (PA if idx % 2 == 0 else PB)
                    o, i = divmod(idx, 8)
                    v.tensor_copy(out=tern_bf[i][:, o * 128:(o + 1) * 128],
                                  in_=ps[:, 0:128]).then_inc(SEM["stV"], 1)

            @blk0.scalar
            def _(s):
                s.wait_ge(SEM["stV"], 25)
                s.activation(out=nAc[:], in_=Ac[:], func=AF.Exp)\
                 .then_inc(SEM["stS"], 1)

            @blk0.tensor
            def _(t):
                for o in range(8):
                    t.wait_ge(SEM["wtP0" if o % 2 == 0 else "wtP1"],
                              16 * (o // 2 + 1))
                    for i in range(8):
                        idx = 8 * o + i
                        if idx >= 2:
                            t.wait_ge(SEM["stV"], 25 + idx - 1)
                        ps = (PA if idx % 2 == 0 else PB)
                        t.transpose(ps[:, 0:128],
                                    wot[o % 2][:, i * 128:(i + 1) * 128], ident[:])\
                         .then_inc(SEM["stP"], 1)

            @blk0.gpsimd
            def _(g):
                g.wait_ge(SEM["stV"], 25)
                g.dma_start(
                    out=wt_s.rearrange("(s ph fh) fl -> ph s fh fl", s=4, fh=2),
                    in_=cmb.rearrange("p (s fh fl) -> p s fh fl", s=4, fh=2))\
                 .then_inc(SEM["stG"], 16)

        with contextlib.ExitStack() as es1:
            E1 = es1.enter_context
            x4 = [[E1(nc.sbuf_tensor(f"x4_{s_}_{i}", [128, D], F32))
                   for i in range(4)] for s_ in range(2)]
            cpr = [E1(nc.sbuf_tensor(f"cpr{i}", [128, D], F32)) for i in range(4)]
            stat = E1(nc.sbuf_tensor("stat", [128, 12], F32))
            sq_scr = E1(nc.sbuf_tensor("sq_scr", [128, D], F32))
            xlnT = [E1(nc.sbuf_tensor(f"xlnT{d}", [128, CH], BF16)) for d in range(NDB)]
            vfm = [E1(nc.sbuf_tensor(f"vfm{i}", [128, CH], F32)) for i in range(2)]
            xp = [E1(nc.sbuf_tensor(f"xp{d}", [128, CH + 3], F32)) for d in range(NDB)]
            xcv = [E1(nc.sbuf_tensor(f"xcv{i}", [128, CH], F32)) for i in range(2)]
            szt = [E1(nc.sbuf_tensor(f"szt{i}", [128, CH], F32)) for i in range(2)]
            xsf = [E1(nc.sbuf_tensor(f"xsf{i}", [128, CH], F32)) for i in range(2)]
            xsb = [E1(nc.sbuf_tensor(f"xsb{d}", [128, CH], BF16)) for d in range(NDB)]
            dtf = [E1(nc.sbuf_tensor(f"dtf{i}", [128, CH], F32)) for i in range(2)]
            te_t = E1(nc.sbuf_tensor("te_t", [128, CH], F32))
            sp_t = [E1(nc.sbuf_tensor(f"sp_t{i}", [128, CH], F32)) for i in range(2)]
            blk1 = E1(nc.Block())

            @blk1.sync
            def _(sy):
                for c in range(NCH):
                    if c >= 2:
                        w(sy, "v1a", c - 1)
                    for i in range(4):
                        sy.dma_start(
                            out=x4[c % 2][i][:],
                            in_=x_h[c * CH + i * 128: c * CH + (i + 1) * 128, :])\
                          .then_inc(SEM["xls"], 16)

            @blk1.vector
            def _(v):
                w(v, "stS", 1)
                v.tensor_scalar(out=Ac[:], in0=nAc[:], scalar1=-1.0, scalar2=None,
                                op0=ALU.mult)
                for d in range(NDB):
                    v.memset(xp[d][:, 0:3], 0.0)
                for c in range(NCH):
                    xst = x4[c % 2]
                    w(v, "xls", 64 * (c + 1))
                    for i in range(4):
                        v.reduce_sum(out=stat[:, i:i + 1], in_=xst[i][:], axis=AX.X)
                    for i in range(4):
                        inst = v.tensor_scalar(out=cpr[i][:], in0=xst[i][:],
                                               scalar1=1024.0,
                                               scalar2=stat[:, i:i + 1],
                                               op0=ALU.mult, op1=ALU.subtract)
                    inst.then_inc(SEM["v1a"], 1)
                    w(v, "s1a", c + 1)
                    for i in range(4):
                        inst = v.tensor_scalar(out=cpr[i][:], in0=cpr[i][:],
                                               scalar1=stat[:, 8 + i:9 + i],
                                               scalar2=None, op0=ALU.mult)
                    inst.then_inc(SEM["v2a"], 1)
                    for d in range(NDB):
                        w(v, "ptr", 8 * c + d + 1)
                        ps = (PC if d % 2 == 0 else PD)
                        v.tensor_copy(out=xlnT[d][:], in_=ps[:, 0:CH])\
                         .then_inc(SEM["vev"], 1)
                    for m in range(8):
                        w(v, "pwi", 16 * c + m + 1)
                        v.tensor_scalar(out=xp[m][:, 3:3 + CH], in0=slot4(m),
                                        scalar1=binc[:, m:m + 1], scalar2=None,
                                        op0=ALU.add).then_inc(SEM["sxp"], 1)
                    for d in range(NDB):
                        if 8 * c + d >= 2:
                            w(v, "sxs", 8 * c + d - 1)
                        cv = xcv[d % 2]
                        v.tensor_scalar(out=cv[:], in0=xp[d][:, 0:CH],
                                        scalar1=cwc[:, 4 * d:4 * d + 1],
                                        scalar2=None, op0=ALU.mult)
                        for kk in range(1, 4):
                            inst = v.scalar_tensor_tensor(
                                out=cv[:], in0=xp[d][:, kk:kk + CH],
                                scalar=cwc[:, 4 * d + kk:4 * d + kk + 1],
                                in1=cv[:], op0=ALU.mult, op1=ALU.add)
                        inst.then_inc(SEM["pcv"], 1)
                        v.tensor_copy(out=xp[d][:, 0:3],
                                      in_=xp[d][:, CH:CH + 3])\
                         .then_inc(SEM["vhc"], 1)
                        w(v, "sxs", 8 * c + d + 1)
                        v.tensor_copy(out=xsb[d][:], in_=xsf[d % 2][:])\
                         .then_inc(SEM["vxb"], 1)
                    for m in range(8):
                        w(v, "sdt", 8 * c + m + 1)
                        if 8 * c + m >= 2:
                            jj = 8 * c + m - 2
                            w(v, "gdt0" if jj % 2 == 0 else "gdt1",
                              16 * (jj // 2 + 1))
                        v.tensor_scalar(out=dtf[m % 2][:], in0=sp_t[m % 2][:],
                                        scalar1=0.2, scalar2=0.001,
                                        op0=ALU.min, op1=ALU.max)\
                         .then_inc(SEM["vdt"], 1)

            @blk1.scalar
            def _(s):
                def dum():
                    s.activation(out=dum_s[:], in_=ident[:], func=AF.Square)

                for c in range(NCH):
                    w(s, "v1a", c + 1)
                    for i in range(4):
                        s.activation(out=sq_scr[:], in_=cpr[i][:], func=AF.Square,
                                     accum_out=stat[:, 4 + i:5 + i])
                    dum()
                    s.activation(out=stat[:, 8:12], in_=stat[:, 4:8], func=AF.Ln,
                                 scale=float(2.0 ** -30), bias=cstt[:, 0:1])
                    dum()
                    s.activation(out=stat[:, 8:12], in_=stat[:, 8:12],
                                 func=AF.Exp, scale=-0.5, bias=cstt[:, 2:3])\
                     .then_inc(SEM["s1a"], 1)
                    for d in range(NDB):
                        w(s, "ptr", 8 * c + d + 1)
                        w(s, "vev", 8 * c + d + 1)
                        if 8 * c + d >= 2:
                            jj = 8 * c + d - 2
                            w(s, "gvs0" if jj % 2 == 0 else "gvs1",
                              16 * (jj // 2 + 1))
                        ps = (PC if d % 2 == 0 else PD)
                        s.activation(out=vfm[d % 2][:], in_=ps[:, 0:CH],
                                     func=AF.Copy).then_inc(SEM["sev"], 1)
                    for m in range(8, 16):
                        w(s, "pwi", 16 * c + m + 1)
                        if 8 * c + (m - 8) >= 2:
                            jj = 8 * c + (m - 8) - 2
                            w(s, "gsz0" if jj % 2 == 0 else "gsz1",
                              16 * (jj // 2 + 1))
                        s.activation(out=szt[m % 2][:], in_=slot4(m),
                                     func=AF.Silu, bias=binc[:, m:m + 1])\
                         .then_inc(SEM["ssz"], 1)
                    for d in range(NDB):
                        w(s, "pcv", 8 * c + d + 1)
                        if 8 * c + d >= 2:
                            jj = 8 * c + d - 2
                            w(s, "gxs0" if jj % 2 == 0 else "gxs1",
                              16 * (jj // 2 + 1))
                            w(s, "vxb", 8 * c + d - 1)
                        s.activation(out=xsf[d % 2][:], in_=xcv[d % 2][:],
                                     func=AF.Silu,
                                     bias=cbc[:, d:d + 1]).then_inc(SEM["sxs"], 1)
                    for m in range(8):
                        w(s, "pdt", 8 * c + m + 1)
                        if 8 * c + m >= 2:
                            w(s, "vdt", 8 * c + m - 1)
                        s.activation(out=te_t[:], in_=slot4(m), func=AF.Exp,
                                     bias=bdtc[:, m:m + 1])
                        s.activation(out=sp_t[m % 2][:], in_=te_t[:], func=AF.Ln,
                                     bias=cstt[:, 1:2]).then_inc(SEM["sdt"], 1)

            @blk1.tensor
            def _(t):
                for c in range(NCH):
                    w(t, "v2a", c + 1)
                    if c >= 1:
                        w(t, "sxs", 8 * c)
                    for d in range(NDB):
                        if 8 * c + d >= 2:
                            w(t, "sev", 8 * c + d - 1)
                        ps = (PC if d % 2 == 0 else PD)
                        for i in range(4):
                            inst = t.transpose(ps[:, i * 128:(i + 1) * 128],
                                               cpr[i][:, d * 128:(d + 1) * 128],
                                               ident[:])
                        inst.then_inc(SEM["ptr"], 1)
                    w(t, "vev", 8 * (c + 1))
                    w(t, "sev", 8 * (c + 1))
                    for m in range(16):
                        if m >= 4:
                            pm = m - 4
                            if pm < 8:
                                w(t, "sxp", 8 * c + pm + 1)
                            else:
                                w(t, "ssz", 8 * c + (pm - 8) + 1)
                        if c >= 1 and m < 4:
                            w(t, "sdt", 8 * c + (m % 4) - 3)
                        for k in range(8):
                            inst = t.matmul(slot4(m),
                                            Win_bf[k][:, m * 128:(m + 1) * 128],
                                            xlnT[k][:], start=(k == 0),
                                            stop=(k == 7))
                        inst.then_inc(SEM["pwi"], 1)
                    w(t, "vxb", 8 * (c + 1))
                    for m in range(8):
                        if 8 * c + m >= 4:
                            w(t, "sdt", 8 * c + m - 3)
                        w(t, "ssz", 8 * c + 5 + (m % 4))
                        for k in range(8):
                            inst = t.matmul(slot4(m),
                                            Wdt_bf[k][:, m * 128:(m + 1) * 128],
                                            xsb[k][:], start=(k == 0), stop=(k == 7))
                        inst.then_inc(SEM["pdt"], 1)

            @blk1.gpsimd
            def _(g):
                for c in range(NCH):
                    for d in range(NDB):
                        w(g, "sev", 8 * c + d + 1)
                        jj = 8 * c + d
                        g.dma_start(out=v_s[d * 128:(d + 1) * 128,
                                           c * CH:(c + 1) * CH],
                                    in_=vfm[d % 2][:])\
                         .then_inc(SEM["gvs0" if jj % 2 == 0 else "gvs1"], 16)
                    for d in range(NDB):
                        w(g, "ssz", 8 * c + d + 1)
                        jj = 8 * c + d
                        g.dma_start(out=sz_s[d * 128:(d + 1) * 128,
                                            c * CH:(c + 1) * CH],
                                    in_=szt[d % 2][:])\
                         .then_inc(SEM["gsz0" if jj % 2 == 0 else "gsz1"], 16)
                    for d in range(NDB):
                        w(g, "sxs", 8 * c + d + 1)
                        jj = 8 * c + d
                        g.dma_start(out=xs_s[d * 128:(d + 1) * 128,
                                            c * CH:(c + 1) * CH],
                                    in_=xsf[d % 2][:])\
                         .then_inc(SEM["gxs0" if jj % 2 == 0 else "gxs1"], 16)
                    for d in range(NDB):
                        w(g, "vdt", 8 * c + d + 1)
                        jj = 8 * c + d
                        g.dma_start(out=dt_s[d * 128:(d + 1) * 128,
                                            c * CH:(c + 1) * CH],
                                    in_=dtf[d % 2][:])\
                         .then_inc(SEM["gdt0" if jj % 2 == 0 else "gdt1"], 16)

        es01.close()

        with contextlib.ExitStack() as es2:
            E2 = es2.enter_context
            dtw = [E2(nc.sbuf_tensor(f"dtw{i}", [128, W], F32)) for i in range(2)]
            xsw = [E2(nc.sbuf_tensor(f"xsw{i}", [128, W], F32)) for i in range(2)]
            qqw = E2(nc.sbuf_tensor("qqw", [128, W], F32))
            pqw = E2(nc.sbuf_tensor("pqw", [128, W], F32))
            dtcw = E2(nc.sbuf_tensor("dtcw", [128, W], F32))
            a_t = [E2(nc.sbuf_tensor(f"a2_{i}", [128, W], F32)) for i in range(2)]
            pi_t = [E2(nc.sbuf_tensor(f"pi2_{i}", [128, W], F32)) for i in range(2)]
            p_t = [E2(nc.sbuf_tensor(f"p2_{i}", [128, W], F32)) for i in range(2)]
            u_t = E2(nc.sbuf_tensor("u2", [128, W], F32))
            z_t = [E2(nc.sbuf_tensor(f"z2_{i}", [128, W], F32)) for i in range(2)]
            lo_t = E2(nc.sbuf_tensor("lo2", [128, W], F32))
            hi_t = E2(nc.sbuf_tensor("hi2", [128, W], F32))
            d_t = [E2(nc.sbuf_tensor(f"d2_{i}", [128, W], F32)) for i in range(2)]
            pd_t = E2(nc.sbuf_tensor("pd2", [128, W], F32))
            e_t = [E2(nc.sbuf_tensor(f"e2_{i}", [128, W], F32)) for i in range(2)]
            dgCa = [E2(nc.sbuf_tensor(f"dgCa{n}", [128, 128], F32))
                    for n in range(NST)]
            dgBa = [E2(nc.sbuf_tensor(f"dgBa{n}", [128, 128], F32))
                    for n in range(NST)]

            carry = E2(nc.sbuf_tensor("carry", [128, 32], F32))
            yw = [E2(nc.sbuf_tensor("yw0", [128, W], F32))] * 2
            blk2 = E2(nc.Block())

            @blk2.sync
            def _(sy):
                sy.wait_ge(SEM["gdt0"], 16 * 32)
                sy.wait_ge(SEM["gdt1"], 16 * 32)
                sy.wait_ge(SEM["gxs0"], 16 * 32)
                sy.wait_ge(SEM["gxs1"], 16 * 32)
                for gw in range(NGW):
                    db, wi = divmod(gw, NWIN)
                    if gw >= 2:
                        w(sy, "visem", 32 * (gw - 1))
                        w(sy, "ssem", 32 * (gw - 1))
                        w(sy, "vwsem", gw - 1)
                    sy.dma_start(out=dtw[gw % 2][:],
                                 in_=dt_s[db * 128:(db + 1) * 128,
                                          wi * W:(wi + 1) * W])\
                      .then_inc(SEM["p2l"], 16)
                    sy.dma_start(out=xsw[gw % 2][:],
                                 in_=xs_s[db * 128:(db + 1) * 128,
                                          wi * W:(wi + 1) * W])\
                      .then_inc(SEM["p2l"], 16)

            @blk2.vector
            def _(v):
                for gw in range(NGW):
                    db, wi = divmod(gw, NWIN)
                    w(v, "p2l", 32 * (gw + 1))
                    w(v, "ssem", 32 * gw)
                    dtc, xsc = dtw[gw % 2], xsw[gw % 2]
                    if wi == 0:
                        w(v, "psem", 32 * gw)
                        for n in range(NST):
                            cn = slice(db * 32 + n, db * 32 + n + 1)
                            v.tensor_scalar(out=dgCa[n][:], in0=ident[:],
                                            scalar1=wCc[:, cn],
                                            scalar2=None, op0=ALU.mult)
                            v.tensor_scalar(out=dgBa[n][:], in0=ident[:],
                                            scalar1=bCc[:, cn],
                                            scalar2=None, op0=ALU.mult)
                    v.tensor_tensor(out=qqw[:], in0=dtc[:], in1=xsc[:], op=ALU.mult)
                    v.tensor_tensor(out=pqw[:], in0=qqw[:], in1=xsc[:], op=ALU.mult)
                    v.tensor_tensor_scan(out=dtcw[:], data0=dtc[:],
                                         data1=dtc[:], initial=0.0,
                                         op0=ALU.add, op1=ALU.max)\
                     .then_inc(SEM["vpre"], 1)
                    for n in range(NST):
                        k = gw * 32 + n
                        cset = k % 2
                        col = slice(db * 32 + n, db * 32 + n + 1)
                        w(v, "ssem", k + 1)
                        if k >= 1:
                            w(v, "psem", k - 1)
                        v.tensor_scalar(out=u_t[:], in0=pqw[:],
                                        scalar1=wBc[:, col], scalar2=None,
                                        op0=ALU.mult)
                        v.scalar_tensor_tensor(out=u_t[:], in0=qqw[:],
                                               scalar=bBc[:, col], in1=u_t[:],
                                               op0=ALU.mult, op1=ALU.add)
                        init = 0.0 if wi == 0 else carry[:, n:n + 1]
                        v.tensor_tensor_scan(out=z_t[cset][:], data0=a_t[cset][:],
                                             data1=u_t[:], initial=init,
                                             op0=ALU.mult, op1=ALU.add)
                        v.scalar_tensor_tensor(out=lo_t[:], in0=z_t[cset][:],
                                               scalar=-10.0, in1=pi_t[cset][:],
                                               op0=ALU.add, op1=ALU.mult)
                        v.scalar_tensor_tensor(out=hi_t[:], in0=z_t[cset][:],
                                               scalar=10.0, in1=pi_t[cset][:],
                                               op0=ALU.add, op1=ALU.mult)
                        v.tensor_tensor_scan(out=d_t[cset][:], data0=lo_t[:],
                                             data1=hi_t[:], initial=0.0,
                                             op0=ALU.max, op1=ALU.min)
                        v.tensor_tensor(out=pd_t[:], in0=p_t[cset][:],
                                        in1=d_t[cset][:], op=ALU.mult)
                        inst = v.tensor_tensor(out=e_t[cset][:], in0=z_t[cset][:],
                                               in1=pd_t[:], op=ALU.subtract)
                        if wi < NWIN - 1:
                            v.tensor_copy(out=dum_v[:, 0:32], in_=ident[:, 0:32])
                            inst = v.scalar_tensor_tensor(
                                out=carry[:, n:n + 1], in0=z_t[cset][:, W - 1:W],
                                scalar=1.0, in1=pd_t[:, W - 1:W],
                                op0=ALU.mult, op1=ALU.subtract)
                        inst.then_inc(SEM["visem"], 1)
                    w(v, "psem", 32 * (gw + 1))
                    if gw >= 1:
                        jj = gw - 1
                        w(v, "gys0" if jj % 2 == 0 else "gys1",
                          16 * (jj // 2 + 1))
                    y1ps = (PA if gw % 2 == 0 else PC)
                    y2ps = (PB if gw % 2 == 0 else PD)
                    v.tensor_tensor(out=yw[gw % 2][:], in0=xsc[:], in1=y1ps[:],
                                    op=ALU.mult)
                    v.tensor_tensor(out=yw[gw % 2][:], in0=yw[gw % 2][:],
                                    in1=y2ps[:], op=ALU.add)\
                     .then_inc(SEM["vwsem"], 1)

            @blk2.scalar
            def _(s):
                for gw in range(NGW):
                    db, wi = divmod(gw, NWIN)
                    w(s, "vpre", gw + 1)
                    dtc = dtw[gw % 2]
                    for n in range(NST):
                        k = gw * 32 + n
                        cset = k % 2
                        col = slice(db * 32 + n, db * 32 + n + 1)
                        if k >= 2:
                            w(s, "visem", k - 1)
                        s.activation(out=a_t[cset][:], in_=dtc[:], func=AF.Exp,
                                     scale=Ac[:, col])
                        s.activation(out=pi_t[cset][:], in_=dtcw[:], func=AF.Exp,
                                     scale=nAc[:, col])
                        s.activation(out=p_t[cset][:], in_=dtcw[:], func=AF.Exp,
                                     scale=Ac[:, col]).then_inc(SEM["ssem"], 1)

            @blk2.tensor
            def _(t):
                for gw in range(NGW):
                    for n in range(NST):
                        k = gw * 32 + n
                        cset = k % 2
                        t.wait_ge(SEM["visem"], k + 1)
                        if n == 0 and gw > 1:
                            t.wait_ge(SEM["vwsem"], gw - 1)
                        y1ps = (PA if gw % 2 == 0 else PC)
                        y2ps = (PB if gw % 2 == 0 else PD)
                        first, last = (n == 0), (n == NST - 1)
                        t.matmul(y1ps[:, 0:512], dgCa[n][:], e_t[cset][:, 0:512],
                                 start=first, stop=last)
                        t.matmul(y1ps[:, 512:1024], dgCa[n][:],
                                 e_t[cset][:, 512:1024],
                                 start=first, stop=last)
                        t.matmul(y2ps[:, 0:512], dgBa[n][:], e_t[cset][:, 0:512],
                                 start=first, stop=last)
                        t.matmul(y2ps[:, 512:1024], dgBa[n][:],
                                 e_t[cset][:, 512:1024],
                                 start=first, stop=last).then_inc(SEM["psem"], 1)

            @blk2.gpsimd
            def _(g):
                for gw in range(NGW):
                    db, wi = divmod(gw, NWIN)
                    w(g, "vwsem", gw + 1)
                    g.dma_start(out=y_s[db * 128:(db + 1) * 128,
                                        wi * W:(wi + 1) * W], in_=yw[gw % 2][:])\
                     .then_inc(SEM["gys0" if gw % 2 == 0 else "gys1"], 16)

        with contextlib.ExitStack() as es3:
            E3 = es3.enter_context
            yfm = [E3(nc.sbuf_tensor(f"yfm{d}", [128, CH], F32)) for d in range(NDB)]
            szf = [E3(nc.sbuf_tensor(f"szf{d}", [128, CH], F32)) for d in range(NDB)]
            vf3 = [E3(nc.sbuf_tensor(f"vf3{d}", [128, CH], F32)) for d in range(NDB)]
            x43 = [[E3(nc.sbuf_tensor(f"x43_{s_}_{i}", [128, D], F32))
                    for i in range(4)] for s_ in range(2)]
            sob = [E3(nc.sbuf_tensor(f"sob{d}", [128, CH], BF16)) for d in range(NDB)]
            hfm = [E3(nc.sbuf_tensor(f"hfm{d}", [128, CH], F32)) for d in range(NDB)]
            h2f = [E3(nc.sbuf_tensor(f"h2f{i}", [128, CH], F32)) for i in range(2)]
            tmp3 = E3(nc.sbuf_tensor("tmp3", [128, CH], F32))
            st3 = E3(nc.sbuf_tensor("st3", [128, 28], F32))
            c3 = [E3(nc.sbuf_tensor(f"c3_{i}", [128, D], F32)) for i in range(4)]
            qi = E3(nc.sbuf_tensor("qi", [128, D], I32))
            qfm = [E3(nc.sbuf_tensor(f"qfm{d}", [128, CH], BF16)) for d in range(NDB)]
            ffm = [E3(nc.sbuf_tensor(f"ffm{d}", [128, CH], F32)) for d in range(NDB)]
            sq3 = E3(nc.sbuf_tensor("sq3", [128, D], F32))
            x2l = c3
            qf = c3
            otl = c3
            blk3 = E3(nc.Block())

            @blk3.sync
            def _(sy):
                sy.wait_ge(SEM["gys0"], 16 * 16)
                sy.wait_ge(SEM["gys1"], 16 * 16)
                sy.wait_ge(SEM["gvs0"], 16 * 32)
                sy.wait_ge(SEM["gvs1"], 16 * 32)
                sy.wait_ge(SEM["gsz0"], 16 * 32)
                sy.wait_ge(SEM["gsz1"], 16 * 32)
                for c in range(NCH):
                    if c >= 1:
                        w(sy, "vso", c)
                    for d in range(NDB):
                        sy.dma_start(out=yfm[d][:],
                                     in_=y_s[d * 128:(d + 1) * 128,
                                             c * CH:(c + 1) * CH])\
                          .then_inc(SEM["p3y"], 16)
                    for d in range(NDB):
                        sy.dma_start(out=szf[d][:],
                                     in_=sz_s[d * 128:(d + 1) * 128,
                                              c * CH:(c + 1) * CH])\
                          .then_inc(SEM["p3s"], 16)
                    if c >= 1:
                        w(sy, "vh", 16 * (c - 1) + 8)
                    for d in range(NDB):
                        sy.dma_start(out=vf3[d][:],
                                     in_=v_s[d * 128:(d + 1) * 128,
                                             c * CH:(c + 1) * CH])\
                          .then_inc(SEM["p3v"], 16)
                    if c >= 2:
                        w(sy, "vfin", 4 * (c - 1))
                    for i in range(4):
                        sy.dma_start(
                            out=x43[c % 2][i][:],
                            in_=x_h[c * CH + i * 128: c * CH + (i + 1) * 128, :])\
                          .then_inc(SEM["p3x"], 16)

            @blk3.vector
            def _(v):
                def dum():
                    v.tensor_copy(out=dum_v[:], in_=ident[:])

                for c in range(NCH):
                    w(v, "p3y", 16 * 8 * (c + 1))
                    w(v, "p3s", 16 * 8 * (c + 1))
                    if c >= 1:
                        w(v, "pwo", 8 * (c + 0))
                    for d in range(NDB):
                        inst = v.tensor_tensor(out=sob[d][:], in0=yfm[d][:],
                                               in1=szf[d][:], op=ALU.mult)
                    inst.then_inc(SEM["vso"], 1)
                    w(v, "p3v", 16 * 8 * (c + 1))
                    for m in range(8):
                        w(v, "pwo", 8 * c + m + 1)
                        v.scalar_tensor_tensor(out=hfm[m][:], in0=slot4(m),
                                               scalar=boutc[:, m:m + 1],
                                               in1=vf3[m][:], op0=ALU.add,
                                               op1=ALU.add).then_inc(SEM["vh"], 1)
                    w(v, "pst", c + 1)
                    v.tensor_scalar(out=tmp3[0:1, 0:CH], in0=PC[0:1, 0:CH],
                                    scalar1=float(2.0 ** -10), scalar2=None,
                                    op0=ALU.mult)
                    v.tensor_scalar(out=tmp3[32:33, 0:CH], in0=PC[0:1, 0:CH],
                                    scalar1=0.03125, scalar2=None,
                                    op0=ALU.mult).then_inc(SEM["vx1"], 1)
                    w(v, "sst", 2 * c + 1)
                    w(v, "psq", 8 * (c + 1))
                    v.tensor_tensor(out=tmp3[32:33, 0:CH],
                                    in0=PC[0:1, CH:2 * CH],
                                    in1=tmp3[64:65, 0:CH],
                                    op=ALU.subtract).then_inc(SEM["vx1"], 1)
                    w(v, "pbc", c + 1)
                    for d in range(NDB):
                        v.tensor_tensor(out=tmp3[:], in0=hfm[d][:],
                                        in1=PD[:, 0:CH], op=ALU.subtract)
                        v.tensor_tensor(out=tmp3[:], in0=tmp3[:],
                                        in1=PD[:, CH:2 * CH], op=ALU.mult)
                        v.tensor_scalar(out=hfm[d][:], in0=tmp3[:],
                                        scalar1=gc[:, d:d + 1],
                                        scalar2=lbc[:, d:d + 1],
                                        op0=ALU.mult, op1=ALU.add)\
                         .then_inc(SEM["vh"], 1)
                    w(v, "p3x", 16 * 4 * (c + 1))
                    for i in range(4):
                        w(v, "pqt", 12 * c + i + 1)
                        v.tensor_tensor(out=x43[c % 2][i][:], in0=PC[:],
                                        in1=x43[c % 2][i][:],
                                        op=ALU.add).then_inc(SEM["vx1"], 1)
                    for i in range(4):
                        v.reduce_sum(out=st3[:, i:i + 1], in_=x43[c % 2][i][:],
                                     axis=AX.X)
                    for i in range(4):
                        inst = v.tensor_scalar(out=c3[i][:], in0=x43[c % 2][i][:],
                                               scalar1=1024.0,
                                               scalar2=st3[:, i:i + 1],
                                               op0=ALU.mult, op1=ALU.subtract)
                    inst.then_inc(SEM["v3s"], 1)
                    w(v, "s3r", c + 1)
                    for i in range(4):
                        v.tensor_scalar(out=x2l[i][:], in0=c3[i][:],
                                        scalar1=st3[:, 20 + i:21 + i],
                                        scalar2=None, op0=ALU.mult)
                    for i in range(4):
                        v.reduce_max(out=st3[:, 4 + i:5 + i], in_=x2l[i][:],
                                     axis=AX.X, apply_absolute_value=True)
                    for i in range(4):
                        v.tensor_tensor(out=x43[c % 2][i][:],
                                        in0=x43[c % 2][i][:],
                                        in1=fbb[:], op=ALU.add)
                    v.tensor_scalar(out=st3[:, 4:8], in0=st3[:, 4:8],
                                    scalar1=1e-5, scalar2=None, op0=ALU.max)
                    dum()
                    dum()
                    v.reciprocal(out=st3[:, 8:12], in_=st3[:, 4:8])
                    v.tensor_scalar(out=st3[:, 12:16], in0=st3[:, 4:8],
                                    scalar1=float(1.0 / 127.0), scalar2=None,
                                    op0=ALU.mult)
                    dum()
                    v.tensor_scalar(out=st3[:, 8:12], in0=st3[:, 8:12],
                                    scalar1=127.0, scalar2=None, op0=ALU.mult)
                    dum()
                    dum()
                    for i in range(4):
                        v.tensor_scalar(out=qf[i][:], in0=x2l[i][:],
                                        scalar1=st3[:, 8 + i:9 + i],
                                        scalar2=None, op0=ALU.mult)
                        v.tensor_copy(out=qi[:], in_=qf[i][:])
                        inst = v.tensor_copy(out=qf[i][:], in_=qi[:])
                    inst.then_inc(SEM["vq"], 1)
                    v.tensor_scalar(out=st3[:, 12:16], in0=st3[:, 12:16],
                                    scalar1=wsc[:, 0:1], scalar2=None,
                                    op0=ALU.mult)
                    for i in range(4):
                        w(v, "pft", 4 * c + i + 1)
                        v.tensor_scalar(out=otl[i][:], in0=PD[:],
                                        scalar1=st3[:, 12 + i:13 + i],
                                        scalar2=None, op0=ALU.mult)
                        v.tensor_tensor(out=otl[i][:], in0=otl[i][:],
                                        in1=x43[c % 2][i][:], op=ALU.add)\
                         .then_inc(SEM["vfin"], 1)

            @blk3.scalar
            def _(s):
                def dum():
                    s.activation(out=dum_s[:], in_=ident[:], func=AF.Square)

                for c in range(NCH):
                    for m in range(8):
                        w(s, "vh", 16 * c + m + 1)
                        if 8 * c + m >= 2:
                            w(s, "psq", 8 * c + m - 1)
                        s.activation(out=h2f[m % 2][:], in_=hfm[m][:],
                                     func=AF.Square).then_inc(SEM["sh2"], 1)
                    w(s, "vx1", 6 * c + 1)
                    s.activation(out=tmp3[64:65, 0:CH],
                                 in_=tmp3[32:33, 0:CH], func=AF.Square)\
                     .then_inc(SEM["sst"], 1)
                    w(s, "vx1", 6 * c + 2)
                    s.activation(out=tmp3[32:33, 0:CH],
                                 in_=tmp3[32:33, 0:CH], func=AF.Ln,
                                 scale=float(2.0 ** -10), bias=cstt[0:1, 0:1])
                    s.activation(out=tmp3[32:33, 0:CH],
                                 in_=tmp3[32:33, 0:CH], func=AF.Exp,
                                 scale=-0.5).then_inc(SEM["sst"], 1)
                    w(s, "v3s", c + 1)
                    for i in range(4):
                        s.activation(out=sq3[:], in_=c3[i][:], func=AF.Square,
                                     accum_out=st3[:, 16 + i:17 + i])
                    dum()
                    s.activation(out=st3[:, 20:24], in_=st3[:, 16:20], func=AF.Ln,
                                 scale=float(2.0 ** -30), bias=cstt[:, 0:1])
                    dum()
                    s.activation(out=st3[:, 20:24], in_=st3[:, 20:24], func=AF.Exp,
                                 scale=-0.5, bias=cstt[:, 2:3]).then_inc(SEM["s3r"], 1)
                    for d in range(NDB):
                        w(s, "pqt", 12 * c + 4 + d + 1)
                        ps = (PA if d % 2 == 0 else PB)
                        s.activation(out=qfm[d][:], in_=ps[:, 0:CH], func=AF.Copy)\
                         .then_inc(SEM["sqe"], 1)
                    for m in range(8):
                        w(s, "pff", 8 * c + m + 1)
                        if c >= 1 and m < 1:
                            w(s, "pft", 4 * c)
                        s.activation(out=ffm[m][:], in_=slot4(m), func=AF.Copy)\
                         .then_inc(SEM["sff"], 1)

            @blk3.tensor
            def _(t):
                for c in range(NCH):
                    t.wait_ge(SEM["vso"], c + 1)
                    if c >= 1:
                        t.wait_ge(SEM["sff"], 8 * c)
                    for m in range(8):
                        if 8 * c + m >= 4:
                            t.wait_ge(SEM["vh"], 16 * c + m - 3)
                        for k in range(8):
                            inst = t.matmul(slot4(m),
                                            Wout_bf[k][:, m * 128:(m + 1) * 128],
                                            sob[k][:], start=(k == 0), stop=(k == 7))
                        inst.then_inc(SEM["pwo"], 1)
                    for d in range(NDB):
                        t.wait_ge(SEM["vh"], 16 * c + d + 1)
                        inst = t.matmul(PC[0:1, 0:CH], ones_t[:, 0:1], hfm[d][:],
                                        start=(d == 0), stop=(d == 7))
                    inst.then_inc(SEM["pst"], 1)
                    for d in range(NDB):
                        t.wait_ge(SEM["sh2"], 8 * c + d + 1)
                        inst = t.matmul(PC[0:1, CH:2 * CH], ones_t[:, 0:1],
                                        h2f[d % 2][:], start=(d == 0),
                                        stop=(d == 7))
                        inst.then_inc(SEM["psq"], 1)
                    t.wait_ge(SEM["sst"], 2 * (c + 1))
                    t.matmul(PD[:, 0:CH], ones_t[0:1, 0:128], tmp3[0:1, 0:CH],
                             start=True, stop=True)
                    t.matmul(PD[:, CH:2 * CH], ones_t[32:33, 0:128],
                             tmp3[32:33, 0:CH], start=True, stop=True)\
                     .then_inc(SEM["pbc"], 1)
                    t.wait_ge(SEM["vh"], 16 * c + 16)
                    for i in range(4):
                        t.wait_ge(SEM["vx1"], 6 * c + 2 + i)
                        for d in range(NDB):
                            inst = t.transpose(PC[:, d * 128:(d + 1) * 128],
                                               hfm[d][:, i * 128:(i + 1) * 128],
                                               ident[:])
                        inst.then_inc(SEM["pqt"], 1)
                    t.wait_ge(SEM["vq"], c + 1)
                    for d in range(NDB):
                        if 8 * c + d >= 2:
                            t.wait_ge(SEM["sqe"], 8 * c + d - 1)
                        ps = (PA if d % 2 == 0 else PB)
                        for i in range(4):
                            inst = t.transpose(ps[:, i * 128:(i + 1) * 128],
                                               qf[i][:, d * 128:(d + 1) * 128],
                                               ident[:])
                        inst.then_inc(SEM["pqt"], 1)
                    t.wait_ge(SEM["sqe"], 8 * (c + 1))
                    for m in range(8):
                        if 8 * c + m >= 4:
                            t.wait_ge(SEM["sff"], 8 * c + m - 3)
                        for k in range(8):
                            inst = t.matmul(slot4(m),
                                            tern_bf[k][:, m * 128:(m + 1) * 128],
                                            qfm[k][:], start=(k == 0), stop=(k == 7))
                        inst.then_inc(SEM["pff"], 1)
                    t.wait_ge(SEM["sff"], 8 * (c + 1))
                    for i in range(4):
                        if 4 * c + i >= 1:
                            t.wait_ge(SEM["vfin"], 4 * c + i)
                        for d in range(NDB):
                            inst = t.transpose(PD[:, d * 128:(d + 1) * 128],
                                               ffm[d][:, i * 128:(i + 1) * 128],
                                               ident[:])
                        inst.then_inc(SEM["pft"], 1)

            @blk3.gpsimd
            def _(g):
                for c in range(NCH):
                    for i in range(4):
                        w(g, "vfin", 4 * c + i + 1)
                        g.dma_start(
                            out=out_h[c * CH + i * 128: c * CH + (i + 1) * 128, :],
                            in_=otl[i][:]).then_inc(SEM["gout"], 16)
                g.wait_ge(SEM["gout"], 16 * 4 * NCH)

    return nc


_CACHE = {}


def kernel(**inputs):
    x = np.asarray(inputs["L0_attn_x"], dtype=np.float32)
    W_in = np.ascontiguousarray(np.asarray(inputs["L0_attn_W_in"], np.float32))
    W_dt = np.ascontiguousarray(np.asarray(inputs["L0_attn_W_dt"], np.float32))
    W_out = np.ascontiguousarray(np.asarray(inputs["L0_attn_W_out"], np.float32))

    def colsplit(vec, nb):
        return np.ascontiguousarray(
            np.asarray(vec, dtype=np.float32).reshape(nb, 128).T)

    def colsplit3(m):
        a = np.asarray(m, dtype=np.float32).reshape(NDB, 128, -1)
        return np.ascontiguousarray(
            a.transpose(1, 0, 2).reshape(128, -1))

    consts = dict(
        W_in=W_in, W_dt=W_dt, W_out=W_out,
        b_in_c=colsplit(inputs["L0_attn_b_in"], 16),
        b_dt_c=colsplit(inputs["L0_attn_b_dt"], 8),
        b_out_c=colsplit(inputs["L0_attn_b_out"], 8),
        conv_b_c=colsplit(inputs["L0_attn_conv_b"], 8),
        ln_g_c=colsplit(inputs["L0_attn_ln_g"], 8),
        ln_b_c=colsplit(inputs["L0_attn_ln_b"], 8),
        conv_w_c=colsplit3(np.asarray(inputs["L0_attn_conv_w"],
                                      np.float32).reshape(D, KC)),
        A_log_c=colsplit3(inputs["L0_attn_A_log"]),
        wB_c=colsplit3(inputs["L0_attn_wB"]),
        bB_c=colsplit3(inputs["L0_attn_bB"]),
        wC_c=colsplit3(inputs["L0_attn_wC"]),
        bC_c=colsplit3(inputs["L0_attn_bC"]),
        packed_c=np.ascontiguousarray(
            np.asarray(inputs["L0_ffn_packed_weight"],
                       np.int32).reshape(128, 2048)),
        wscale=np.repeat(np.asarray(inputs["L0_ffn_weight_scale"],
                                    np.float32).reshape(1, 1), 128, axis=0),
        ffn_bias=np.repeat(np.asarray(inputs["L0_ffn_bias"],
                                      np.float32).reshape(1, D), 128, axis=0),
        ident=np.eye(128, dtype=np.float32),
    )

    if "nc" not in _CACHE:
        _CACHE["nc"] = build_nc()
    nc = _CACHE["nc"]

    in_maps = [dict(consts, x=np.ascontiguousarray(x[b])) for b in range(B)]
    res = run_bass_kernel_spmd(nc, in_maps, core_ids=list(range(B)))
    out = np.stack([np.asarray(res.results[b]["out"]) for b in range(B)], axis=0)
    return out.astype(np.float32)


# version 17
# speedup vs baseline: 1.0205x; 1.0027x over previous
import contextlib

import numpy as np

import concourse.bass as bass
import concourse.mybir as mybir
from concourse.bass_utils import run_bass_kernel_spmd

F32 = mybir.dt.float32
BF16 = mybir.dt.bfloat16
I32 = mybir.dt.int32
AF = mybir.ActivationFunctionType
ALU = mybir.AluOpType
AX = mybir.AxisListType

B, S, D, NST, KC = 8, 4096, 1024, 32, 4
NDB = D // 128
CH = 512
NCH = S // CH
W = 1024
NWIN = S // W
NGW = NDB * NWIN
LN1024 = float(np.log(1024.0))
EPS = 1e-5


def build_nc():
    nc = bass.Bass("TRN2", target_bir_lowering=False, debug=False)

    x_h = nc.dram_tensor("x", [S, D], F32, kind="ExternalInput").ap()
    Win_h = nc.dram_tensor("W_in", [D, 2 * D], F32, kind="ExternalInput").ap()
    Wdt_h = nc.dram_tensor("W_dt", [D, D], F32, kind="ExternalInput").ap()
    Wout_h = nc.dram_tensor("W_out", [D, D], F32, kind="ExternalInput").ap()
    binc_h = nc.dram_tensor("b_in_c", [128, 16], F32, kind="ExternalInput").ap()
    bdtc_h = nc.dram_tensor("b_dt_c", [128, 8], F32, kind="ExternalInput").ap()
    boutc_h = nc.dram_tensor("b_out_c", [128, 8], F32, kind="ExternalInput").ap()
    cbc_h = nc.dram_tensor("conv_b_c", [128, 8], F32, kind="ExternalInput").ap()
    gc_h = nc.dram_tensor("ln_g_c", [128, 8], F32, kind="ExternalInput").ap()
    lbc_h = nc.dram_tensor("ln_b_c", [128, 8], F32, kind="ExternalInput").ap()
    cw_h = nc.dram_tensor("conv_w_c", [128, 32], F32, kind="ExternalInput").ap()
    alog_h = nc.dram_tensor("A_log_c", [128, 256], F32, kind="ExternalInput").ap()
    wB_h = nc.dram_tensor("wB_c", [128, 256], F32, kind="ExternalInput").ap()
    bB_h = nc.dram_tensor("bB_c", [128, 256], F32, kind="ExternalInput").ap()
    wC_h = nc.dram_tensor("wC_c", [128, 256], F32, kind="ExternalInput").ap()
    bC_h = nc.dram_tensor("bC_c", [128, 256], F32, kind="ExternalInput").ap()
    pk_h = nc.dram_tensor("packed_c", [128, 2048], I32, kind="ExternalInput").ap()
    mk_h = nc.dram_tensor("masks_c", [128, 4], I32, kind="ExternalInput").ap()
    ws_h = nc.dram_tensor("wscale", [128, 1], F32, kind="ExternalInput").ap()
    fb_h = nc.dram_tensor("ffn_bias", [128, D], F32, kind="ExternalInput").ap()
    id_h = nc.dram_tensor("ident", [128, 128], F32, kind="ExternalInput").ap()

    out_h = nc.dram_tensor("out", [S, D], F32, kind="ExternalOutput").ap()

    dt_s = nc.dram_tensor("dt_s", [D, S], F32, kind="Internal").ap()
    xs_s = nc.dram_tensor("xs_s", [D, S], F32, kind="Internal").ap()
    sz_s = nc.dram_tensor("sz_s", [D, S], F32, kind="Internal").ap()
    v_s = nc.dram_tensor("v_s", [D, S], F32, kind="Internal").ap()
    y_s = nc.dram_tensor("y_s", [D, S], F32, kind="Internal").ap()
    wt_s = nc.dram_tensor("wt_s", [D, D], F32, kind="Internal").ap()

    es = contextlib.ExitStack()
    with es:
        E = es.enter_context

        Wout_bf = [E(nc.sbuf_tensor(f"Woutb{k}", [128, D], BF16)) for k in range(8)]
        tern_bf = [E(nc.sbuf_tensor(f"ternb{k}", [128, D], BF16)) for k in range(8)]
        ident = E(nc.sbuf_tensor("identt", [128, 128], F32))
        ones_t = E(nc.sbuf_tensor("ones_t", [128, 128], F32))
        dum_s = E(nc.sbuf_tensor("dum_s", [128, 128], F32))
        dum_v = E(nc.sbuf_tensor("dum_v", [128, 128], F32))
        binc = E(nc.sbuf_tensor("binc", [128, 16], F32))
        bdtc = E(nc.sbuf_tensor("bdtc", [128, 8], F32))
        boutc = E(nc.sbuf_tensor("boutc", [128, 8], F32))
        cbc = E(nc.sbuf_tensor("cbc", [128, 8], F32))
        gc = E(nc.sbuf_tensor("gc", [128, 8], F32))
        lbc = E(nc.sbuf_tensor("lbc", [128, 8], F32))
        cwc = E(nc.sbuf_tensor("cwc", [128, 32], F32))
        Ac = E(nc.sbuf_tensor("Ac", [128, 256], F32))
        nAc = E(nc.sbuf_tensor("nAc", [128, 256], F32))
        wBc = E(nc.sbuf_tensor("wBc", [128, 256], F32))
        bBc = E(nc.sbuf_tensor("bBc", [128, 256], F32))
        wCc = E(nc.sbuf_tensor("wCc", [128, 256], F32))
        bCc = E(nc.sbuf_tensor("bCc", [128, 256], F32))
        wsc = E(nc.sbuf_tensor("wsc", [128, 4], F32))
        fbb = E(nc.sbuf_tensor("fbb", [128, D], F32))
        cstt = E(nc.sbuf_tensor("cstt", [128, 4], F32))
        fbrow = E(nc.sbuf_tensor("fbrow", [128, D], F32))

        PA = E(nc.psum_tensor("PA", [128, 1024], F32))
        PB = E(nc.psum_tensor("PB", [128, 1024], F32))
        PC = E(nc.psum_tensor("PC", [128, 1024], F32))
        PD = E(nc.psum_tensor("PD", [128, 1024], F32))

        semnames = ["stL", "stV", "stS", "stP", "stG", "wgL0", "wgL1",
                    "wtP0", "wtP1",
                    "xls", "v1a", "s1a", "v2a", "ptr", "vev", "sev", "gvs",
                    "pwi", "sxp", "ssz", "gsz", "pcv", "sxs", "gxs", "vxb", "vhc",
                    "pdt", "sdt", "vdt", "gdt",
                    "gvs0", "gvs1", "gsz0", "gsz1", "gxs0", "gxs1",
                    "gdt0", "gdt1", "gys0", "gys1",
                    "p3y", "p3s", "p3v", "p3x", "vsc",
                    "p2l", "vpre", "ssem", "visem", "psem", "plsem", "pusem", "vwsem", "gys",
                    "p3l", "vso", "pwo", "vh", "sh2", "psq", "pst", "sst",
                    "pbc", "vx1", "v3s", "s3r", "vq", "pqt", "sqe", "pff",
                    "sff", "pft", "vfin", "gout"]
        SEM = {n: E(nc.semaphore(n)) for n in semnames}

        def w(eng, sem, val):
            if val > 0:
                eng.wait_ge(SEM[sem], val)

        PSLOT = None

        def slot4(m):
            return [PA[:, 0:CH], PA[:, CH:2 * CH],
                    PB[:, 0:CH], PB[:, CH:2 * CH]][m % 4]

        es01 = es.enter_context(contextlib.ExitStack())
        E01 = es01.enter_context
        Win_bf = [E01(nc.sbuf_tensor(f"Winb{k}", [128, 2 * D], BF16))
                  for k in range(8)]
        Wdt_bf = [E01(nc.sbuf_tensor(f"Wdtb{k}", [128, D], BF16))
                  for k in range(8)]

        with contextlib.ExitStack() as es0:
            E0 = es0.enter_context
            wstg = [E0(nc.sbuf_tensor(f"wstg{i}", [128, 2 * D], F32)) for i in range(2)]
            pk_t = E0(nc.sbuf_tensor("pk_t", [128, 2048], I32))
            msk = E0(nc.sbuf_tensor("msk", [128, 4], I32))
            es_t = E0(nc.sbuf_tensor("es_t", [128, 2048], I32))
            cmb = E0(nc.sbuf_tensor("cmb", [128, 8192], F32))
            wot = [E0(nc.sbuf_tensor(f"wot{i}", [128, D], F32)) for i in range(2)]
            blk0 = E0(nc.Block())

            @blk0.sync
            def _(sy):
                loads = [(Win_h[k * 128:(k + 1) * 128, :], 2 * D) for k in range(8)]
                loads += [(Wdt_h[k * 128:(k + 1) * 128, :], D) for k in range(8)]
                loads += [(Wout_h[k * 128:(k + 1) * 128, :], D) for k in range(8)]
                for i, (src, width) in enumerate(loads):
                    if i >= 2:
                        w(sy, "stV", i - 1)
                    sy.dma_start(out=wstg[i % 2][:, 0:width], in_=src)\
                      .then_inc(SEM["wgL0" if i % 2 == 0 else "wgL1"], 16)
                for dst, src in [(binc, binc_h), (bdtc, bdtc_h), (boutc, boutc_h),
                                 (cbc, cbc_h), (gc, gc_h), (lbc, lbc_h),
                                 (cwc, cw_h), (Ac, alog_h), (wBc, wB_h),
                                 (bBc, bB_h), (wCc, wC_h), (bCc, bC_h)]:
                    sy.dma_start(out=dst[:], in_=src[:]).then_inc(SEM["stL"], 16)
                sy.dma_start(out=pk_t[:], in_=pk_h[:]).then_inc(SEM["stL"], 16)
                sy.dma_start(out=msk[:], in_=mk_h[:]).then_inc(SEM["stL"], 16)
                sy.dma_start(out=wsc[:, 0:1], in_=ws_h[:]).then_inc(SEM["stL"], 16)
                sy.dma_start(out=fbb[:], in_=fb_h[:]).then_inc(SEM["stL"], 16)
                sy.dma_start(out=ident[:], in_=id_h[:]).then_inc(SEM["stL"], 16)
                sy.wait_ge(SEM["stG"], 16)
                for o in range(8):
                    if o >= 2:
                        w(sy, "stP", 8 * (o - 1))
                    sy.dma_start(out=wot[o % 2][:],
                                 in_=wt_s[o * 128:(o + 1) * 128, :])\
                      .then_inc(SEM["wtP0" if o % 2 == 0 else "wtP1"], 16)

            @blk0.vector
            def _(v):
                tgts = ([(Win_bf[k], 2 * D) for k in range(8)]
                        + [(Wdt_bf[k], D) for k in range(8)]
                        + [(Wout_bf[k], D) for k in range(8)])
                for i, (dst, width) in enumerate(tgts):
                    w(v, "wgL0" if i % 2 == 0 else "wgL1", 16 * (i // 2 + 1))
                    v.tensor_copy(out=dst[:], in_=wstg[i % 2][:, 0:width])\
                     .then_inc(SEM["stV"], 1)
                w(v, "stL", 16 * 17)
                v.memset(ones_t[:], 1.0)
                v.memset(dum_v[:], 0.0)
                v.memset(cstt[:, 0:1], EPS)
                v.memset(cstt[:, 1:2], 1.0)
                v.memset(cstt[:, 2:3], -LN1024)
                v.tensor_scalar(out=Ac[:], in0=Ac[:], scalar1=2.0, scalar2=-5.0,
                                op0=ALU.min, op1=ALU.max)
                for sft in range(4):
                    v.tensor_scalar(out=es_t[:], in0=pk_t[:],
                                    scalar1=msk[:, sft:sft + 1], scalar2=None,
                                    op0=ALU.bitwise_and)
                    inst = v.tensor_scalar(
                        out=cmb[:, sft * 2048:(sft + 1) * 2048],
                        in0=es_t[:], scalar1=float(0.25 ** sft), scalar2=-1.0,
                        op0=ALU.mult, op1=ALU.add)
                inst.then_inc(SEM["stV"], 1)
                for idx in range(64):
                    w(v, "stP", idx + 1)
                    ps = (PA if idx % 2 == 0 else PB)
                    o, i = divmod(idx, 8)
                    v.tensor_copy(out=tern_bf[i][:, o * 128:(o + 1) * 128],
                                  in_=ps[:, 0:128]).then_inc(SEM["stV"], 1)

            @blk0.scalar
            def _(s):
                s.wait_ge(SEM["stV"], 25)
                s.activation(out=nAc[:], in_=Ac[:], func=AF.Exp)\
                 .then_inc(SEM["stS"], 1)

            @blk0.tensor
            def _(t):
                for o in range(8):
                    t.wait_ge(SEM["wtP0" if o % 2 == 0 else "wtP1"],
                              16 * (o // 2 + 1))
                    for i in range(8):
                        idx = 8 * o + i
                        if idx >= 2:
                            t.wait_ge(SEM["stV"], 25 + idx - 1)
                        ps = (PA if idx % 2 == 0 else PB)
                        t.transpose(ps[:, 0:128],
                                    wot[o % 2][:, i * 128:(i + 1) * 128], ident[:])\
                         .then_inc(SEM["stP"], 1)

            @blk0.gpsimd
            def _(g):
                g.wait_ge(SEM["stV"], 25)
                g.dma_start(
                    out=wt_s.rearrange("(s ph fh) fl -> ph s fh fl", s=4, fh=2),
                    in_=cmb.rearrange("p (s fh fl) -> p s fh fl", s=4, fh=2))\
                 .then_inc(SEM["stG"], 16)

        with contextlib.ExitStack() as es1:
            E1 = es1.enter_context
            x4 = [[E1(nc.sbuf_tensor(f"x4_{s_}_{i}", [128, D], F32))
                   for i in range(4)] for s_ in range(2)]
            cpr = [E1(nc.sbuf_tensor(f"cpr{i}", [128, D], F32)) for i in range(4)]
            stat = E1(nc.sbuf_tensor("stat", [128, 12], F32))
            sq_scr = E1(nc.sbuf_tensor("sq_scr", [128, D], F32))
            xlnT = [E1(nc.sbuf_tensor(f"xlnT{d}", [128, CH], BF16)) for d in range(NDB)]
            vfm = [E1(nc.sbuf_tensor(f"vfm{i}", [128, CH], F32)) for i in range(2)]
            xp = [E1(nc.sbuf_tensor(f"xp{d}", [128, CH + 3], F32)) for d in range(NDB)]
            xcv = [E1(nc.sbuf_tensor(f"xcv{i}", [128, CH], F32)) for i in range(2)]
            szt = [E1(nc.sbuf_tensor(f"szt{i}", [128, CH], F32)) for i in range(2)]
            xsf = [E1(nc.sbuf_tensor(f"xsf{i}", [128, CH], F32)) for i in range(2)]
            xsb = [E1(nc.sbuf_tensor(f"xsb{d}", [128, CH], BF16)) for d in range(NDB)]
            dtf = [E1(nc.sbuf_tensor(f"dtf{i}", [128, CH], F32)) for i in range(2)]
            te_t = E1(nc.sbuf_tensor("te_t", [128, CH], F32))
            sp_t = [E1(nc.sbuf_tensor(f"sp_t{i}", [128, CH], F32)) for i in range(2)]
            blk1 = E1(nc.Block())

            @blk1.sync
            def _(sy):
                for c in range(NCH):
                    if c >= 2:
                        w(sy, "v1a", c - 1)
                    for i in range(4):
                        sy.dma_start(
                            out=x4[c % 2][i][:],
                            in_=x_h[c * CH + i * 128: c * CH + (i + 1) * 128, :])\
                          .then_inc(SEM["xls"], 16)

            @blk1.vector
            def _(v):
                w(v, "stS", 1)
                v.tensor_scalar(out=Ac[:], in0=nAc[:], scalar1=-1.0, scalar2=None,
                                op0=ALU.mult)
                for d in range(NDB):
                    v.memset(xp[d][:, 0:3], 0.0)
                for c in range(NCH):
                    xst = x4[c % 2]
                    w(v, "xls", 64 * (c + 1))
                    for i in range(4):
                        inst = v.reduce_sum(out=stat[:, i:i + 1], in_=xst[i][:],
                                            axis=AX.X)
                    inst.then_inc(SEM["vsc"], 1)
                    w(v, "vsc", c + 1)
                    for i in range(4):
                        inst = v.tensor_scalar(out=cpr[i][:], in0=xst[i][:],
                                               scalar1=1024.0,
                                               scalar2=stat[:, i:i + 1],
                                               op0=ALU.mult, op1=ALU.subtract)
                    inst.then_inc(SEM["v1a"], 1)
                    w(v, "s1a", c + 1)
                    for i in range(4):
                        inst = v.tensor_scalar(out=cpr[i][:], in0=cpr[i][:],
                                               scalar1=stat[:, 8 + i:9 + i],
                                               scalar2=None, op0=ALU.mult)
                    inst.then_inc(SEM["v2a"], 1)
                    for d in range(NDB):
                        w(v, "ptr", 8 * c + d + 1)
                        ps = (PC if d % 2 == 0 else PD)
                        v.tensor_copy(out=xlnT[d][:], in_=ps[:, 0:CH])\
                         .then_inc(SEM["vev"], 1)
                    for m in range(8):
                        w(v, "pwi", 16 * c + m + 1)
                        v.tensor_scalar(out=xp[m][:, 3:3 + CH], in0=slot4(m),
                                        scalar1=binc[:, m:m + 1], scalar2=None,
                                        op0=ALU.add).then_inc(SEM["sxp"], 1)
                    for d in range(NDB):
                        if 8 * c + d >= 2:
                            w(v, "sxs", 8 * c + d - 1)
                        cv = xcv[d % 2]
                        v.tensor_scalar(out=cv[:], in0=xp[d][:, 0:CH],
                                        scalar1=cwc[:, 4 * d:4 * d + 1],
                                        scalar2=None, op0=ALU.mult)
                        for kk in range(1, 4):
                            inst = v.scalar_tensor_tensor(
                                out=cv[:], in0=xp[d][:, kk:kk + CH],
                                scalar=cwc[:, 4 * d + kk:4 * d + kk + 1],
                                in1=cv[:], op0=ALU.mult, op1=ALU.add)
                        inst.then_inc(SEM["pcv"], 1)
                        v.tensor_copy(out=xp[d][:, 0:3],
                                      in_=xp[d][:, CH:CH + 3])\
                         .then_inc(SEM["vhc"], 1)
                        w(v, "sxs", 8 * c + d + 1)
                        v.tensor_copy(out=xsb[d][:], in_=xsf[d % 2][:])\
                         .then_inc(SEM["vxb"], 1)
                    for m in range(8):
                        w(v, "sdt", 8 * c + m + 1)
                        if 8 * c + m >= 2:
                            jj = 8 * c + m - 2
                            w(v, "gdt0" if jj % 2 == 0 else "gdt1",
                              16 * (jj // 2 + 1))
                        v.tensor_scalar(out=dtf[m % 2][:], in0=sp_t[m % 2][:],
                                        scalar1=0.2, scalar2=0.001,
                                        op0=ALU.min, op1=ALU.max)\
                         .then_inc(SEM["vdt"], 1)

            @blk1.scalar
            def _(s):
                def dum():
                    s.activation(out=dum_s[:], in_=ident[:], func=AF.Square)

                for c in range(NCH):
                    w(s, "v1a", c + 1)
                    for i in range(4):
                        s.activation(out=sq_scr[:], in_=cpr[i][:], func=AF.Square,
                                     accum_out=stat[:, 4 + i:5 + i])
                    dum()
                    s.activation(out=stat[:, 8:12], in_=stat[:, 4:8], func=AF.Ln,
                                 scale=float(2.0 ** -30), bias=cstt[:, 0:1])
                    dum()
                    s.activation(out=stat[:, 8:12], in_=stat[:, 8:12],
                                 func=AF.Exp, scale=-0.5, bias=cstt[:, 2:3])\
                     .then_inc(SEM["s1a"], 1)
                    for d in range(NDB):
                        w(s, "ptr", 8 * c + d + 1)
                        w(s, "vev", 8 * c + d + 1)
                        if 8 * c + d >= 2:
                            jj = 8 * c + d - 2
                            w(s, "gvs0" if jj % 2 == 0 else "gvs1",
                              16 * (jj // 2 + 1))
                        ps = (PC if d % 2 == 0 else PD)
                        s.activation(out=vfm[d % 2][:], in_=ps[:, 0:CH],
                                     func=AF.Copy).then_inc(SEM["sev"], 1)
                    for m in range(8, 16):
                        w(s, "pwi", 16 * c + m + 1)
                        if 8 * c + (m - 8) >= 2:
                            jj = 8 * c + (m - 8) - 2
                            w(s, "gsz0" if jj % 2 == 0 else "gsz1",
                              16 * (jj // 2 + 1))
                        s.activation(out=szt[m % 2][:], in_=slot4(m),
                                     func=AF.Silu, bias=binc[:, m:m + 1])\
                         .then_inc(SEM["ssz"], 1)
                    for d in range(NDB):
                        w(s, "pcv", 8 * c + d + 1)
                        if 8 * c + d >= 2:
                            jj = 8 * c + d - 2
                            w(s, "gxs0" if jj % 2 == 0 else "gxs1",
                              16 * (jj // 2 + 1))
                            w(s, "vxb", 8 * c + d - 1)
                        s.activation(out=xsf[d % 2][:], in_=xcv[d % 2][:],
                                     func=AF.Silu,
                                     bias=cbc[:, d:d + 1]).then_inc(SEM["sxs"], 1)
                    for m in range(8):
                        w(s, "pdt", 8 * c + m + 1)
                        if 8 * c + m >= 2:
                            w(s, "vdt", 8 * c + m - 1)
                        s.activation(out=te_t[:], in_=slot4(m), func=AF.Exp,
                                     bias=bdtc[:, m:m + 1])
                        s.activation(out=sp_t[m % 2][:], in_=te_t[:], func=AF.Ln,
                                     bias=cstt[:, 1:2]).then_inc(SEM["sdt"], 1)

            @blk1.tensor
            def _(t):
                for c in range(NCH):
                    w(t, "v2a", c + 1)
                    if c >= 1:
                        w(t, "sxs", 8 * c)
                    for d in range(NDB):
                        if 8 * c + d >= 2:
                            w(t, "sev", 8 * c + d - 1)
                        ps = (PC if d % 2 == 0 else PD)
                        for i in range(4):
                            inst = t.transpose(ps[:, i * 128:(i + 1) * 128],
                                               cpr[i][:, d * 128:(d + 1) * 128],
                                               ident[:])
                        inst.then_inc(SEM["ptr"], 1)
                    w(t, "vev", 8 * (c + 1))
                    w(t, "sev", 8 * (c + 1))
                    for m in range(16):
                        if m >= 4:
                            pm = m - 4
                            if pm < 8:
                                w(t, "sxp", 8 * c + pm + 1)
                            else:
                                w(t, "ssz", 8 * c + (pm - 8) + 1)
                        if c >= 1 and m < 4:
                            w(t, "sdt", 8 * c + (m % 4) - 3)
                        for k in range(8):
                            inst = t.matmul(slot4(m),
                                            Win_bf[k][:, m * 128:(m + 1) * 128],
                                            xlnT[k][:], start=(k == 0),
                                            stop=(k == 7))
                        inst.then_inc(SEM["pwi"], 1)
                    w(t, "vxb", 8 * (c + 1))
                    for m in range(8):
                        if 8 * c + m >= 4:
                            w(t, "sdt", 8 * c + m - 3)
                        w(t, "ssz", 8 * c + 5 + (m % 4))
                        for k in range(8):
                            inst = t.matmul(slot4(m),
                                            Wdt_bf[k][:, m * 128:(m + 1) * 128],
                                            xsb[k][:], start=(k == 0), stop=(k == 7))
                        inst.then_inc(SEM["pdt"], 1)

            @blk1.gpsimd
            def _(g):
                for c in range(NCH):
                    for d in range(NDB):
                        w(g, "sev", 8 * c + d + 1)
                        jj = 8 * c + d
                        g.dma_start(out=v_s[d * 128:(d + 1) * 128,
                                           c * CH:(c + 1) * CH],
                                    in_=vfm[d % 2][:])\
                         .then_inc(SEM["gvs0" if jj % 2 == 0 else "gvs1"], 16)
                    for d in range(NDB):
                        w(g, "ssz", 8 * c + d + 1)
                        jj = 8 * c + d
                        g.dma_start(out=sz_s[d * 128:(d + 1) * 128,
                                            c * CH:(c + 1) * CH],
                                    in_=szt[d % 2][:])\
                         .then_inc(SEM["gsz0" if jj % 2 == 0 else "gsz1"], 16)
                    for d in range(NDB):
                        w(g, "sxs", 8 * c + d + 1)
                        jj = 8 * c + d
                        g.dma_start(out=xs_s[d * 128:(d + 1) * 128,
                                            c * CH:(c + 1) * CH],
                                    in_=xsf[d % 2][:])\
                         .then_inc(SEM["gxs0" if jj % 2 == 0 else "gxs1"], 16)
                    for d in range(NDB):
                        w(g, "vdt", 8 * c + d + 1)
                        jj = 8 * c + d
                        g.dma_start(out=dt_s[d * 128:(d + 1) * 128,
                                            c * CH:(c + 1) * CH],
                                    in_=dtf[d % 2][:])\
                         .then_inc(SEM["gdt0" if jj % 2 == 0 else "gdt1"], 16)

        es01.close()

        with contextlib.ExitStack() as es2:
            E2 = es2.enter_context
            dtw = [E2(nc.sbuf_tensor(f"dtw{i}", [128, W], F32)) for i in range(2)]
            xsw = [E2(nc.sbuf_tensor(f"xsw{i}", [128, W], F32)) for i in range(2)]
            qqw = E2(nc.sbuf_tensor("qqw", [128, W], F32))
            pqw = E2(nc.sbuf_tensor("pqw", [128, W], F32))
            dtcw = E2(nc.sbuf_tensor("dtcw", [128, W], F32))
            a_t = [E2(nc.sbuf_tensor(f"a2_{i}", [128, W], F32)) for i in range(2)]
            pi_t = [E2(nc.sbuf_tensor(f"pi2_{i}", [128, W], F32)) for i in range(2)]
            p_t = [E2(nc.sbuf_tensor(f"p2_{i}", [128, W], F32)) for i in range(2)]
            u_t = E2(nc.sbuf_tensor("u2", [128, W], F32))
            z_t = [E2(nc.sbuf_tensor(f"z2_{i}", [128, W], F32)) for i in range(2)]
            lo_t = E2(nc.sbuf_tensor("lo2", [128, W], F32))
            hi_t = E2(nc.sbuf_tensor("hi2", [128, W], F32))
            d_t = [E2(nc.sbuf_tensor(f"d2_{i}", [128, W], F32)) for i in range(2)]
            pd_t = E2(nc.sbuf_tensor("pd2", [128, W], F32))
            e_t = [E2(nc.sbuf_tensor(f"e2_{i}", [128, W], F32)) for i in range(2)]
            dgCa = [E2(nc.sbuf_tensor(f"dgCa{n}", [128, 128], F32))
                    for n in range(NST)]
            dgBa = [E2(nc.sbuf_tensor(f"dgBa{n}", [128, 128], F32))
                    for n in range(NST)]

            carry = E2(nc.sbuf_tensor("carry", [128, 32], F32))
            yw = [E2(nc.sbuf_tensor("yw0", [128, W], F32))] * 2
            blk2 = E2(nc.Block())

            @blk2.sync
            def _(sy):
                sy.wait_ge(SEM["gdt0"], 16 * 32)
                sy.wait_ge(SEM["gdt1"], 16 * 32)
                sy.wait_ge(SEM["gxs0"], 16 * 32)
                sy.wait_ge(SEM["gxs1"], 16 * 32)
                for gw in range(NGW):
                    db, wi = divmod(gw, NWIN)
                    if gw >= 2:
                        w(sy, "visem", 32 * (gw - 1))
                        w(sy, "ssem", 32 * (gw - 1))
                        w(sy, "vwsem", gw - 1)
                    sy.dma_start(out=dtw[gw % 2][:],
                                 in_=dt_s[db * 128:(db + 1) * 128,
                                          wi * W:(wi + 1) * W])\
                      .then_inc(SEM["p2l"], 16)
                    sy.dma_start(out=xsw[gw % 2][:],
                                 in_=xs_s[db * 128:(db + 1) * 128,
                                          wi * W:(wi + 1) * W])\
                      .then_inc(SEM["p2l"], 16)

            @blk2.vector
            def _(v):
                for gw in range(NGW):
                    db, wi = divmod(gw, NWIN)
                    w(v, "p2l", 32 * (gw + 1))
                    w(v, "ssem", 32 * gw)
                    dtc, xsc = dtw[gw % 2], xsw[gw % 2]
                    if wi == 0:
                        w(v, "psem", 32 * gw)
                        for n in range(NST):
                            cn = slice(db * 32 + n, db * 32 + n + 1)
                            v.tensor_scalar(out=dgCa[n][:], in0=ident[:],
                                            scalar1=wCc[:, cn],
                                            scalar2=None, op0=ALU.mult)
                            v.tensor_scalar(out=dgBa[n][:], in0=ident[:],
                                            scalar1=bCc[:, cn],
                                            scalar2=None, op0=ALU.mult)
                    v.tensor_tensor(out=qqw[:], in0=dtc[:], in1=xsc[:], op=ALU.mult)
                    v.tensor_tensor(out=pqw[:], in0=qqw[:], in1=xsc[:], op=ALU.mult)
                    v.tensor_tensor_scan(out=dtcw[:], data0=dtc[:],
                                         data1=dtc[:], initial=0.0,
                                         op0=ALU.add, op1=ALU.max)\
                     .then_inc(SEM["vpre"], 1)
                    for n in range(NST):
                        k = gw * 32 + n
                        cset = k % 2
                        col = slice(db * 32 + n, db * 32 + n + 1)
                        w(v, "ssem", k + 1)
                        if k >= 1:
                            w(v, "psem", k - 1)
                        v.tensor_scalar(out=u_t[:], in0=pqw[:],
                                        scalar1=wBc[:, col], scalar2=None,
                                        op0=ALU.mult)
                        v.scalar_tensor_tensor(out=u_t[:], in0=qqw[:],
                                               scalar=bBc[:, col], in1=u_t[:],
                                               op0=ALU.mult, op1=ALU.add)
                        init = 0.0 if wi == 0 else carry[:, n:n + 1]
                        v.tensor_tensor_scan(out=z_t[cset][:], data0=a_t[cset][:],
                                             data1=u_t[:], initial=init,
                                             op0=ALU.mult, op1=ALU.add)
                        v.scalar_tensor_tensor(out=lo_t[:], in0=z_t[cset][:],
                                               scalar=-10.0, in1=pi_t[cset][:],
                                               op0=ALU.add, op1=ALU.mult)
                        v.scalar_tensor_tensor(out=hi_t[:], in0=z_t[cset][:],
                                               scalar=10.0, in1=pi_t[cset][:],
                                               op0=ALU.add, op1=ALU.mult)
                        v.tensor_tensor_scan(out=d_t[cset][:], data0=lo_t[:],
                                             data1=hi_t[:], initial=0.0,
                                             op0=ALU.max, op1=ALU.min)
                        v.tensor_tensor(out=pd_t[:], in0=p_t[cset][:],
                                        in1=d_t[cset][:], op=ALU.mult)
                        inst = v.tensor_tensor(out=e_t[cset][:], in0=z_t[cset][:],
                                               in1=pd_t[:], op=ALU.subtract)
                        if wi < NWIN - 1:
                            v.tensor_copy(out=dum_v[:, 0:32], in_=ident[:, 0:32])
                            inst = v.scalar_tensor_tensor(
                                out=carry[:, n:n + 1], in0=z_t[cset][:, W - 1:W],
                                scalar=1.0, in1=pd_t[:, W - 1:W],
                                op0=ALU.mult, op1=ALU.subtract)
                        inst.then_inc(SEM["visem"], 1)
                    w(v, "psem", 32 * (gw + 1))
                    if gw >= 1:
                        jj = gw - 1
                        w(v, "gys0" if jj % 2 == 0 else "gys1",
                          16 * (jj // 2 + 1))
                    y1ps = (PA if gw % 2 == 0 else PC)
                    y2ps = (PB if gw % 2 == 0 else PD)
                    v.tensor_tensor(out=yw[gw % 2][:], in0=xsc[:], in1=y1ps[:],
                                    op=ALU.mult)
                    v.tensor_tensor(out=yw[gw % 2][:], in0=yw[gw % 2][:],
                                    in1=y2ps[:], op=ALU.add)\
                     .then_inc(SEM["vwsem"], 1)

            @blk2.scalar
            def _(s):
                for gw in range(NGW):
                    db, wi = divmod(gw, NWIN)
                    w(s, "vpre", gw + 1)
                    dtc = dtw[gw % 2]
                    for n in range(NST):
                        k = gw * 32 + n
                        cset = k % 2
                        col = slice(db * 32 + n, db * 32 + n + 1)
                        if k >= 2:
                            w(s, "visem", k - 1)
                        s.activation(out=a_t[cset][:], in_=dtc[:], func=AF.Exp,
                                     scale=Ac[:, col])
                        s.activation(out=pi_t[cset][:], in_=dtcw[:], func=AF.Exp,
                                     scale=nAc[:, col])
                        s.activation(out=p_t[cset][:], in_=dtcw[:], func=AF.Exp,
                                     scale=Ac[:, col]).then_inc(SEM["ssem"], 1)

            @blk2.tensor
            def _(t):
                for gw in range(NGW):
                    for n in range(NST):
                        k = gw * 32 + n
                        cset = k % 2
                        t.wait_ge(SEM["visem"], k + 1)
                        if n == 0 and gw > 1:
                            t.wait_ge(SEM["vwsem"], gw - 1)
                        y1ps = (PA if gw % 2 == 0 else PC)
                        y2ps = (PB if gw % 2 == 0 else PD)
                        first, last = (n == 0), (n == NST - 1)
                        t.matmul(y1ps[:, 0:512], dgCa[n][:], e_t[cset][:, 0:512],
                                 start=first, stop=last)
                        t.matmul(y1ps[:, 512:1024], dgCa[n][:],
                                 e_t[cset][:, 512:1024],
                                 start=first, stop=last)
                        t.matmul(y2ps[:, 0:512], dgBa[n][:], e_t[cset][:, 0:512],
                                 start=first, stop=last)
                        t.matmul(y2ps[:, 512:1024], dgBa[n][:],
                                 e_t[cset][:, 512:1024],
                                 start=first, stop=last).then_inc(SEM["psem"], 1)

            @blk2.gpsimd
            def _(g):
                for gw in range(NGW):
                    db, wi = divmod(gw, NWIN)
                    w(g, "vwsem", gw + 1)
                    g.dma_start(out=y_s[db * 128:(db + 1) * 128,
                                        wi * W:(wi + 1) * W], in_=yw[gw % 2][:])\
                     .then_inc(SEM["gys0" if gw % 2 == 0 else "gys1"], 16)

        with contextlib.ExitStack() as es3:
            E3 = es3.enter_context
            yfm = [E3(nc.sbuf_tensor(f"yfm{d}", [128, CH], F32)) for d in range(NDB)]
            szf = [E3(nc.sbuf_tensor(f"szf{d}", [128, CH], F32)) for d in range(NDB)]
            vf3 = [E3(nc.sbuf_tensor(f"vf3{d}", [128, CH], F32)) for d in range(NDB)]
            x43 = [[E3(nc.sbuf_tensor(f"x43_{s_}_{i}", [128, D], F32))
                    for i in range(4)] for s_ in range(2)]
            sob = [E3(nc.sbuf_tensor(f"sob{d}", [128, CH], BF16)) for d in range(NDB)]
            hfm = [E3(nc.sbuf_tensor(f"hfm{d}", [128, CH], F32)) for d in range(NDB)]
            h2f = [E3(nc.sbuf_tensor(f"h2f{i}", [128, CH], F32)) for i in range(2)]
            tmp3 = E3(nc.sbuf_tensor("tmp3", [128, CH], F32))
            st3 = E3(nc.sbuf_tensor("st3", [128, 28], F32))
            c3 = [E3(nc.sbuf_tensor(f"c3_{i}", [128, D], F32)) for i in range(4)]
            qi = E3(nc.sbuf_tensor("qi", [128, D], I32))
            qfm = [E3(nc.sbuf_tensor(f"qfm{d}", [128, CH], BF16)) for d in range(NDB)]
            ffm = [E3(nc.sbuf_tensor(f"ffm{d}", [128, CH], F32)) for d in range(NDB)]
            sq3 = E3(nc.sbuf_tensor("sq3", [128, D], F32))
            x2l = c3
            qf = c3
            otl = c3
            blk3 = E3(nc.Block())

            @blk3.sync
            def _(sy):
                sy.wait_ge(SEM["gys0"], 16 * 16)
                sy.wait_ge(SEM["gys1"], 16 * 16)
                sy.wait_ge(SEM["gvs0"], 16 * 32)
                sy.wait_ge(SEM["gvs1"], 16 * 32)
                sy.wait_ge(SEM["gsz0"], 16 * 32)
                sy.wait_ge(SEM["gsz1"], 16 * 32)
                for c in range(NCH):
                    if c >= 1:
                        w(sy, "vso", c)
                    for d in range(NDB):
                        sy.dma_start(out=yfm[d][:],
                                     in_=y_s[d * 128:(d + 1) * 128,
                                             c * CH:(c + 1) * CH])\
                          .then_inc(SEM["p3y"], 16)
                    for d in range(NDB):
                        sy.dma_start(out=szf[d][:],
                                     in_=sz_s[d * 128:(d + 1) * 128,
                                              c * CH:(c + 1) * CH])\
                          .then_inc(SEM["p3s"], 16)
                    if c >= 1:
                        w(sy, "vh", 16 * (c - 1) + 8)
                    for d in range(NDB):
                        sy.dma_start(out=vf3[d][:],
                                     in_=v_s[d * 128:(d + 1) * 128,
                                             c * CH:(c + 1) * CH])\
                          .then_inc(SEM["p3v"], 16)
                    if c >= 2:
                        w(sy, "vfin", 4 * (c - 1))
                    for i in range(4):
                        sy.dma_start(
                            out=x43[c % 2][i][:],
                            in_=x_h[c * CH + i * 128: c * CH + (i + 1) * 128, :])\
                          .then_inc(SEM["p3x"], 16)

            @blk3.vector
            def _(v):
                def dum():
                    v.tensor_copy(out=dum_v[:], in_=ident[:])

                for c in range(NCH):
                    w(v, "p3y", 16 * 8 * (c + 1))
                    w(v, "p3s", 16 * 8 * (c + 1))
                    if c >= 1:
                        w(v, "pwo", 8 * (c + 0))
                    for d in range(NDB):
                        inst = v.tensor_tensor(out=sob[d][:], in0=yfm[d][:],
                                               in1=szf[d][:], op=ALU.mult)
                    inst.then_inc(SEM["vso"], 1)
                    w(v, "p3v", 16 * 8 * (c + 1))
                    for m in range(8):
                        w(v, "pwo", 8 * c + m + 1)
                        v.scalar_tensor_tensor(out=hfm[m][:], in0=slot4(m),
                                               scalar=boutc[:, m:m + 1],
                                               in1=vf3[m][:], op0=ALU.add,
                                               op1=ALU.add).then_inc(SEM["vh"], 1)
                    w(v, "pst", c + 1)
                    v.tensor_scalar(out=tmp3[0:1, 0:CH], in0=PC[0:1, 0:CH],
                                    scalar1=float(2.0 ** -10), scalar2=None,
                                    op0=ALU.mult)
                    v.tensor_scalar(out=tmp3[32:33, 0:CH], in0=PC[0:1, 0:CH],
                                    scalar1=0.03125, scalar2=None,
                                    op0=ALU.mult).then_inc(SEM["vx1"], 1)
                    w(v, "sst", 2 * c + 1)
                    w(v, "psq", 8 * (c + 1))
                    v.tensor_tensor(out=tmp3[32:33, 0:CH],
                                    in0=PC[0:1, CH:2 * CH],
                                    in1=tmp3[64:65, 0:CH],
                                    op=ALU.subtract).then_inc(SEM["vx1"], 1)
                    w(v, "pbc", c + 1)
                    for d in range(NDB):
                        v.tensor_tensor(out=tmp3[:], in0=hfm[d][:],
                                        in1=PD[:, 0:CH], op=ALU.subtract)
                        v.tensor_tensor(out=tmp3[:], in0=tmp3[:],
                                        in1=PD[:, CH:2 * CH], op=ALU.mult)
                        v.tensor_scalar(out=hfm[d][:], in0=tmp3[:],
                                        scalar1=gc[:, d:d + 1],
                                        scalar2=lbc[:, d:d + 1],
                                        op0=ALU.mult, op1=ALU.add)\
                         .then_inc(SEM["vh"], 1)
                    w(v, "p3x", 16 * 4 * (c + 1))
                    for i in range(4):
                        w(v, "pqt", 12 * c + i + 1)
                        v.tensor_tensor(out=x43[c % 2][i][:], in0=PC[:],
                                        in1=x43[c % 2][i][:],
                                        op=ALU.add).then_inc(SEM["vx1"], 1)
                    for i in range(4):
                        inst = v.reduce_sum(out=st3[:, i:i + 1],
                                            in_=x43[c % 2][i][:], axis=AX.X)
                    inst.then_inc(SEM["vsc"], 1)
                    w(v, "vsc", 8 + 4 * c + 1)
                    for i in range(4):
                        inst = v.tensor_scalar(out=c3[i][:], in0=x43[c % 2][i][:],
                                               scalar1=1024.0,
                                               scalar2=st3[:, i:i + 1],
                                               op0=ALU.mult, op1=ALU.subtract)
                    inst.then_inc(SEM["v3s"], 1)
                    w(v, "s3r", c + 1)
                    for i in range(4):
                        v.tensor_scalar(out=x2l[i][:], in0=c3[i][:],
                                        scalar1=st3[:, 20 + i:21 + i],
                                        scalar2=None, op0=ALU.mult)
                    for i in range(4):
                        v.reduce_max(out=st3[:, 4 + i:5 + i], in_=x2l[i][:],
                                     axis=AX.X, apply_absolute_value=True)
                    for i in range(4):
                        v.tensor_tensor(out=x43[c % 2][i][:],
                                        in0=x43[c % 2][i][:],
                                        in1=fbb[:], op=ALU.add)
                    v.tensor_scalar(out=st3[:, 4:8], in0=st3[:, 4:8],
                                    scalar1=1e-5, scalar2=None, op0=ALU.max)
                    dum()
                    dum()
                    v.reciprocal(out=st3[:, 8:12], in_=st3[:, 4:8])
                    v.tensor_scalar(out=st3[:, 12:16], in0=st3[:, 4:8],
                                    scalar1=float(1.0 / 127.0), scalar2=None,
                                    op0=ALU.mult)
                    dum()
                    inst = v.tensor_scalar(out=st3[:, 8:12], in0=st3[:, 8:12],
                                           scalar1=127.0, scalar2=None,
                                           op0=ALU.mult)
                    inst.then_inc(SEM["vsc"], 1)
                    w(v, "vsc", 8 + 4 * c + 2)
                    for i in range(4):
                        v.tensor_scalar(out=qf[i][:], in0=x2l[i][:],
                                        scalar1=st3[:, 8 + i:9 + i],
                                        scalar2=None, op0=ALU.mult)
                        v.tensor_copy(out=qi[:], in_=qf[i][:])
                        inst = v.tensor_copy(out=qf[i][:], in_=qi[:])
                    inst.then_inc(SEM["vq"], 1)
                    inst = v.tensor_scalar(out=st3[:, 12:16], in0=st3[:, 12:16],
                                           scalar1=wsc[:, 0:1], scalar2=None,
                                           op0=ALU.mult)
                    inst.then_inc(SEM["vsc"], 1)
                    w(v, "vsc", 8 + 4 * c + 3)
                    for i in range(4):
                        w(v, "pft", 4 * c + i + 1)
                        inst = v.tensor_scalar(out=otl[i][:], in0=PD[:],
                                               scalar1=st3[:, 12 + i:13 + i],
                                               scalar2=None, op0=ALU.mult)
                        if i == 3:
                            inst.then_inc(SEM["vsc"], 1)
                        inst = v.tensor_tensor(out=otl[i][:], in0=otl[i][:],
                                               in1=x43[c % 2][i][:], op=ALU.add)
                        inst.then_inc(SEM["vfin"], 1)

            @blk3.scalar
            def _(s):
                def dum():
                    s.activation(out=dum_s[:], in_=ident[:], func=AF.Square)

                for c in range(NCH):
                    for m in range(8):
                        w(s, "vh", 16 * c + m + 1)
                        if 8 * c + m >= 2:
                            w(s, "psq", 8 * c + m - 1)
                        s.activation(out=h2f[m % 2][:], in_=hfm[m][:],
                                     func=AF.Square).then_inc(SEM["sh2"], 1)
                    w(s, "vx1", 6 * c + 1)
                    s.activation(out=tmp3[64:65, 0:CH],
                                 in_=tmp3[32:33, 0:CH], func=AF.Square)\
                     .then_inc(SEM["sst"], 1)
                    w(s, "vx1", 6 * c + 2)
                    s.activation(out=tmp3[32:33, 0:CH],
                                 in_=tmp3[32:33, 0:CH], func=AF.Ln,
                                 scale=float(2.0 ** -10), bias=cstt[0:1, 0:1])
                    s.activation(out=tmp3[32:33, 0:CH],
                                 in_=tmp3[32:33, 0:CH], func=AF.Exp,
                                 scale=-0.5).then_inc(SEM["sst"], 1)
                    w(s, "v3s", c + 1)
                    for i in range(4):
                        s.activation(out=sq3[:], in_=c3[i][:], func=AF.Square,
                                     accum_out=st3[:, 16 + i:17 + i])
                    dum()
                    s.activation(out=st3[:, 20:24], in_=st3[:, 16:20], func=AF.Ln,
                                 scale=float(2.0 ** -30), bias=cstt[:, 0:1])
                    dum()
                    s.activation(out=st3[:, 20:24], in_=st3[:, 20:24], func=AF.Exp,
                                 scale=-0.5, bias=cstt[:, 2:3]).then_inc(SEM["s3r"], 1)
                    for d in range(NDB):
                        w(s, "pqt", 12 * c + 4 + d + 1)
                        ps = (PA if d % 2 == 0 else PB)
                        s.activation(out=qfm[d][:], in_=ps[:, 0:CH], func=AF.Copy)\
                         .then_inc(SEM["sqe"], 1)
                    for m in range(8):
                        w(s, "pff", 8 * c + m + 1)
                        if c >= 1 and m < 1:
                            w(s, "pft", 4 * c)
                        s.activation(out=ffm[m][:], in_=slot4(m), func=AF.Copy)\
                         .then_inc(SEM["sff"], 1)

            @blk3.tensor
            def _(t):
                for c in range(NCH):
                    t.wait_ge(SEM["vso"], c + 1)
                    if c >= 1:
                        t.wait_ge(SEM["sff"], 8 * c)
                    for m in range(8):
                        if 8 * c + m >= 4:
                            t.wait_ge(SEM["vh"], 16 * c + m - 3)
                        for k in range(8):
                            inst = t.matmul(slot4(m),
                                            Wout_bf[k][:, m * 128:(m + 1) * 128],
                                            sob[k][:], start=(k == 0), stop=(k == 7))
                        inst.then_inc(SEM["pwo"], 1)
                    for d in range(NDB):
                        t.wait_ge(SEM["vh"], 16 * c + d + 1)
                        inst = t.matmul(PC[0:1, 0:CH], ones_t[:, 0:1], hfm[d][:],
                                        start=(d == 0), stop=(d == 7))
                    inst.then_inc(SEM["pst"], 1)
                    for d in range(NDB):
                        t.wait_ge(SEM["sh2"], 8 * c + d + 1)
                        inst = t.matmul(PC[0:1, CH:2 * CH], ones_t[:, 0:1],
                                        h2f[d % 2][:], start=(d == 0),
                                        stop=(d == 7))
                        inst.then_inc(SEM["psq"], 1)
                    t.wait_ge(SEM["sst"], 2 * (c + 1))
                    t.matmul(PD[:, 0:CH], ones_t[0:1, 0:128], tmp3[0:1, 0:CH],
                             start=True, stop=True)
                    t.matmul(PD[:, CH:2 * CH], ones_t[32:33, 0:128],
                             tmp3[32:33, 0:CH], start=True, stop=True)\
                     .then_inc(SEM["pbc"], 1)
                    t.wait_ge(SEM["vh"], 16 * c + 16)
                    for i in range(4):
                        t.wait_ge(SEM["vx1"], 6 * c + 2 + i)
                        for d in range(NDB):
                            inst = t.transpose(PC[:, d * 128:(d + 1) * 128],
                                               hfm[d][:, i * 128:(i + 1) * 128],
                                               ident[:])
                        inst.then_inc(SEM["pqt"], 1)
                    t.wait_ge(SEM["vq"], c + 1)
                    for d in range(NDB):
                        if 8 * c + d >= 2:
                            t.wait_ge(SEM["sqe"], 8 * c + d - 1)
                        ps = (PA if d % 2 == 0 else PB)
                        for i in range(4):
                            inst = t.transpose(ps[:, i * 128:(i + 1) * 128],
                                               qf[i][:, d * 128:(d + 1) * 128],
                                               ident[:])
                        inst.then_inc(SEM["pqt"], 1)
                    t.wait_ge(SEM["sqe"], 8 * (c + 1))
                    for m in range(8):
                        if 8 * c + m >= 4:
                            t.wait_ge(SEM["sff"], 8 * c + m - 3)
                        for k in range(8):
                            inst = t.matmul(slot4(m),
                                            tern_bf[k][:, m * 128:(m + 1) * 128],
                                            qfm[k][:], start=(k == 0), stop=(k == 7))
                        inst.then_inc(SEM["pff"], 1)
                    t.wait_ge(SEM["sff"], 8 * (c + 1))
                    for i in range(4):
                        if 4 * c + i >= 1:
                            t.wait_ge(SEM["vfin"], 4 * c + i)
                        for d in range(NDB):
                            inst = t.transpose(PD[:, d * 128:(d + 1) * 128],
                                               ffm[d][:, i * 128:(i + 1) * 128],
                                               ident[:])
                        inst.then_inc(SEM["pft"], 1)

            @blk3.gpsimd
            def _(g):
                for c in range(NCH):
                    for i in range(4):
                        w(g, "vfin", 4 * c + i + 1)
                        g.dma_start(
                            out=out_h[c * CH + i * 128: c * CH + (i + 1) * 128, :],
                            in_=otl[i][:]).then_inc(SEM["gout"], 16)
                g.wait_ge(SEM["gout"], 16 * 4 * NCH)

    return nc


_CACHE = {}


def kernel(**inputs):
    x = np.asarray(inputs["L0_attn_x"], dtype=np.float32)
    W_in = np.ascontiguousarray(np.asarray(inputs["L0_attn_W_in"], np.float32))
    W_dt = np.ascontiguousarray(np.asarray(inputs["L0_attn_W_dt"], np.float32))
    W_out = np.ascontiguousarray(np.asarray(inputs["L0_attn_W_out"], np.float32))

    def colsplit(vec, nb):
        return np.ascontiguousarray(
            np.asarray(vec, dtype=np.float32).reshape(nb, 128).T)

    def colsplit3(m):
        a = np.asarray(m, dtype=np.float32).reshape(NDB, 128, -1)
        return np.ascontiguousarray(
            a.transpose(1, 0, 2).reshape(128, -1))

    consts = dict(
        W_in=W_in, W_dt=W_dt, W_out=W_out,
        b_in_c=colsplit(inputs["L0_attn_b_in"], 16),
        b_dt_c=colsplit(inputs["L0_attn_b_dt"], 8),
        b_out_c=colsplit(inputs["L0_attn_b_out"], 8),
        conv_b_c=colsplit(inputs["L0_attn_conv_b"], 8),
        ln_g_c=colsplit(inputs["L0_attn_ln_g"], 8),
        ln_b_c=colsplit(inputs["L0_attn_ln_b"], 8),
        conv_w_c=colsplit3(np.asarray(inputs["L0_attn_conv_w"],
                                      np.float32).reshape(D, KC)),
        A_log_c=colsplit3(inputs["L0_attn_A_log"]),
        wB_c=colsplit3(inputs["L0_attn_wB"]),
        bB_c=colsplit3(inputs["L0_attn_bB"]),
        wC_c=colsplit3(inputs["L0_attn_wC"]),
        bC_c=colsplit3(inputs["L0_attn_bC"]),
        packed_c=np.ascontiguousarray(
            np.asarray(inputs["L0_ffn_packed_weight"],
                       np.int32).reshape(128, 2048)),
        wscale=np.repeat(np.asarray(inputs["L0_ffn_weight_scale"],
                                    np.float32).reshape(1, 1), 128, axis=0),
        ffn_bias=np.repeat(np.asarray(inputs["L0_ffn_bias"],
                                      np.float32).reshape(1, D), 128, axis=0),
        ident=np.eye(128, dtype=np.float32),
        masks_c=np.repeat(np.array([[3, 12, 48, 192]], dtype=np.int32),
                          128, axis=0),
    )

    if "nc" not in _CACHE:
        _CACHE["nc"] = build_nc()
    nc = _CACHE["nc"]

    in_maps = [dict(consts, x=np.ascontiguousarray(x[b])) for b in range(B)]
    res = run_bass_kernel_spmd(nc, in_maps, core_ids=list(range(B)))
    out = np.stack([np.asarray(res.results[b]["out"]) for b in range(B)], axis=0)
    return out.astype(np.float32)
